# version 19
# baseline (speedup 1.0000x reference)
import contextlib
import numpy as np
import concourse.bass as bass
import concourse.mybir as mybir
from concourse.bass_utils import run_bass_kernel_spmd

F32 = mybir.dt.float32
BF16 = mybir.dt.bfloat16
AF = mybir.ActivationFunctionType
ALU = mybir.AluOpType
AX = mybir.AxisListType.X

H = 2048
NHC = 16
S = 4096
T = 2048
EPS = 1e-5
C_SCALE = float(1.0 / np.sqrt(2048.0))

ENGINES = ("tensor", "vector", "scalar", "gpsimd", "sync")

E_P = [0, 16, 40, 64, 88, 96]


class Sem:
    __slots__ = ("h", "v")

    def __init__(self, st, nc, name):
        self.h = st.enter_context(nc.semaphore(name))
        self.v = 0

    def inc(self, instr, n=1):
        instr.then_inc(self.h, n)
        self.v += n


class Builder:
    fdis = ()

    def __init__(self, nc, st, debug=False):
        self.nc = nc
        E = st.enter_context
        dbg = dict(kind="ExternalOutput") if debug else {}
        dt = nc.dram_tensor
        self.x_all = dt("x_all", [T, H], F32, kind="ExternalInput").ap()
        self.wq = dt("wq", [16, 128, 16, 128], F32, kind="ExternalInput").ap()
        self.wk = dt("wk", [16, 128, 16, 128], F32, kind="ExternalInput").ap()
        self.wv = dt("wv", [16, 128, H], F32, kind="ExternalInput").ap()
        self.wgu = dt("wgu", [128, 128, 16, 128], F32, kind="ExternalInput").ap()
        self.wd = dt("wd", [64, 128, H], F32, kind="ExternalInput").ap()
        self.wo = dt("wo", [16, 128, H], F32, kind="ExternalInput").ap()
        self.bqk = dt("bqk", [128, 32], F32, kind="ExternalInput").ap()
        self.bvb_in = dt("bvb_in", [128, H], F32, kind="ExternalInput").ap()
        self.bob_in = dt("bob_in", [128, H], F32, kind="ExternalInput").ap()
        self.out_d = dt("out", [T, H], F32, kind="ExternalOutput").ap()
        self.debug = debug
        self.q_d = dt("q_d", [16, 128, T], BF16).ap()
        self.kv_own = dt("kv_own", [32, 128, H], BF16).ap()
        self.kv_all = dt("kv_all", [2, 32, 128, H], BF16).ap()
        self.h_d = dt("h_d", [64, 128, T], BF16).ap()
        self.g_d = dt("g_d", [16, 128, H], F32, **dbg).ap()
        self.c_d = dt("c_d", [16, 128, T], BF16).ap()
        self.wo_bf_d = dt("wo_bf_d", [16 * 128, H], BF16).ap()
        if debug:
            self.xn_f = dt("xn_f", [128, NHC * T], F32, kind="ExternalOutput").ap()
            self.dbg_outs = {
                "q_f": (dt("q_f", [16, 128, T], F32, kind="ExternalOutput").ap(), self.q_d),
                "kv_f": (dt("kv_f", [2, 32, 128, H], F32, kind="ExternalOutput").ap(), self.kv_all),
                "h_f": (dt("h_f", [64, 128, T], F32, kind="ExternalOutput").ap(), self.h_d),
                "c_f": (dt("c_f", [16, 128, T], F32, kind="ExternalOutput").ap(), self.c_d),
            }
        self.ident = E(nc.sbuf_tensor("ident", [128, 128], BF16))
        self.epsv = E(nc.sbuf_tensor("epsv", [128, 1], F32))
        self.bqk_sb = E(nc.sbuf_tensor("bqk_sb", [128, 32], F32))
        for name in ("bar", "bnA", "rc", "mn", "mset", "wox", "nm0", "nm1", "vl2", "cc",
                     "xld2", "xld3", "xn2", "xn3",
                     "xld0", "xld1", "xn0", "xn1", "mv0", "mv1", "sd0", "sd1",
                     "tp", "tpe", "wld0", "wld1", "misc", "gmm", "bev",
                     "sp0", "sp1", "sp2", "sp3", "silu", "hev",
                     "wd0", "wd1", "wd2", "wd3", "wd4", "wd5", "wd6", "wd7",
                     "dmm", "dev", "gs0", "gs1",
                     "hl0", "hl1", "ql", "kl0", "kl1", "vl0", "vl1",
                     "scmm", "scev", "exp", "eb", "rin", "ptf", "ctf", "ptev",
                     "pvev", "vused", "cs", "wold", "wold2", "bold", "fmm", "fev", "ep",
                     "gl0", "gl1", "xl0", "xl1", "cl0", "cl1", "osp0", "osp1"):
            setattr(self, "s_" + name, Sem(st, nc, "s_" + name))

    def barrier(self, waits):
        with self.nc.Block() as blk:
            for en in ENGINES:
                dec = getattr(blk, en)

                def body(e, en=en):
                    for (s, v) in waits.get(en, ()):
                        e.wait_ge(s.h, v)
                    e.nop().then_inc(self.s_bar.h, 1)

                dec(body)
        self.s_bar.v += 5
        return self.s_bar.v

    def phase_a(self, bt):
        nc = self.nc
        with contextlib.ExitStack() as st:
            E = st.enter_context
            xbuf = [E(nc.sbuf_tensor(f"xbufA{i}", [128, H], F32)) for i in range(4)]
            xn_tm = [E(nc.sbuf_tensor(f"xntmA{i}", [128, H], BF16)) for i in range(2)]
            bn6 = [E(nc.sbuf_tensor(f"bn6A{i}", [128, 24], F32)) for i in range(2)]
            mv = [E(nc.sbuf_tensor(f"mvA{i}", [128, 2], F32)) for i in range(2)]
            sd = [E(nc.sbuf_tensor(f"sdA{i}", [128, 1], F32)) for i in range(2)]
            rstd = [E(nc.sbuf_tensor(f"rstdA{i}", [128, 1], F32)) for i in range(2)]
            nmean = [E(nc.sbuf_tensor(f"nmA{i}", [128, 1], F32)) for i in range(2)]
            pbt = [E(nc.psum_tensor(f"pbtA{i}", [128, 1024], BF16)) for i in range(2)]
            xld = [self.s_xld0, self.s_xld1, self.s_xld2, self.s_xld3]
            xn = [self.s_xn0, self.s_xn1, self.s_xn2, self.s_xn3]
            mvs = [self.s_mv0, self.s_mv1]
            sds = [self.s_sd0, self.s_sd1]

            def evac(eng, tt):
                eng.wait_ge(self.s_tp.h, 2 * tt + 2)
                for a in range(2):
                    self.s_tpe.inc(eng.copy(
                        self.xn_fm[:, a * 8:(a + 1) * 8, tt * 128:(tt + 1) * 128],
                        pbt[a][:, :].rearrange("p (c t) -> p c t", c=8)))

            with nc.Block() as blk:
                @blk.sync
                def _(sy):
                    sy.wait_ge(self.s_bar.h, bt)
                    for tt in range(16):
                        p4, r4 = tt % 4, tt // 4
                        if r4 > 0:
                            sy.wait_ge(xn[p4].h, r4)
                        xld[p4].inc(sy.dma_start(
                            xbuf[p4][:, :],
                            self.x_all[tt * 128:(tt + 1) * 128, :]), 16)

                @blk.gpsimd
                def _(g):
                    g.wait_ge(self.s_bar.h, bt)
                    self.s_mset.inc(g.memset(self.ident[:, :], 0.0))
                    g.wait_ge(self.s_mset.h, 1)
                    g.affine_select(out=self.ident[:, :], in_=self.ident[:, :],
                                    compare_op=ALU.not_equal, fill=1.0, base=0,
                                    pattern=[[-1, 128]], channel_multiplier=1)
                    self.s_mset.inc(g.memset(self.epsv[:, :], EPS))
                    self.s_misc.inc(g.dma_start(self.bqk_sb[:, :], self.bqk), 16)
                    for oc in range(2):
                        wld = [self.s_wld0, self.s_wld1][oc]
                        wld.inc(g.dma_start(
                            self.wbuf[oc][:, 0:2048],
                            self.wq[oc].rearrange("p c o -> p (c o)")), 16)
                    nms = [self.s_nm0, self.s_nm1]
                    for tt in range(16):
                        p, r = tt % 2, tt // 2
                        p4, r4 = tt % 4, tt // 4
                        g.wait_ge(xld[p4].h, 16 * (r4 + 1))
                        g.wait_ge(nms[p].h, r + 1)
                        g.wait_ge(self.s_rc.h, tt + 1)
                        if tt >= 2:
                            g.wait_ge(self.s_tp.h, 2 * (tt - 2) + 2)
                        xn[p4].inc(g.tensor_scalar(
                            xn_tm[p][:, :], xbuf[p4][:, :],
                            nmean[p][:, :], rstd[p][:, :],
                            op0=ALU.add, op1=ALU.mult))

                @blk.vector
                def _(v):
                    v.wait_ge(self.s_bar.h, bt)
                    nms = [self.s_nm0, self.s_nm1]
                    for tt in range(16):
                        p, r = tt % 2, tt // 2
                        p4, r4 = tt % 4, tt // 4
                        v.wait_ge(xld[p4].h, 16 * (r4 + 1))
                        if tt >= 2:
                            v.wait_ge(xn[(tt - 2) % 4].h, (tt - 2) // 4 + 1)
                        for j in range(4):
                            self.s_bnA.inc(v.bn_stats(
                                bn6[p][:, 6 * j:6 * j + 6],
                                xbuf[p4][:, 512 * j:512 * (j + 1)]))
                        v.wait_ge(self.s_bnA.h, 4 * (tt + 1))
                        mvs[p].inc(v.bn_aggr(mv[p][:, :], bn6[p][:, :]))
                        v.wait_ge(mvs[p].h, r + 1)
                        nms[p].inc(v.tensor_scalar(
                            nmean[p][:, :], mv[p][:, 0:1], -1.0, None,
                            op0=ALU.mult))
                        v.wait_ge(sds[p].h, r + 1)
                        self.s_rc.inc(v.reciprocal(rstd[p][:, :], sd[p][:, :]))

                @blk.scalar
                def _(s):
                    s.wait_ge(self.s_bar.h, bt)
                    s.wait_ge(self.s_mset.h, 2)
                    for tt in range(16):
                        p, r = tt % 2, tt // 2
                        s.wait_ge(mvs[p].h, r + 1)
                        sds[p].inc(s.activation(sd[p][:, :], mv[p][:, 1:2],
                                                AF.Sqrt, bias=self.epsv[:, 0:1]))
                        if tt >= 1:
                            evac(s, tt - 1)
                    evac(s, 15)

                @blk.tensor
                def _(t):
                    t.wait_ge(self.s_bar.h, bt)
                    for tt in range(16):
                        p = tt % 2
                        t.wait_ge(xn[tt % 4].h, tt // 4 + 1)
                        if tt >= 1:
                            t.wait_ge(self.s_tpe.h, 2 * tt)
                        for a in range(2):
                            for j in range(8):
                                hc = a * 8 + j
                                tr = t.transpose(
                                    pbt[a][:, j * 128:(j + 1) * 128],
                                    xn_tm[p][:, hc * 128:(hc + 1) * 128],
                                    self.ident[:, :])
                                if j == 7:
                                    self.s_tp.inc(tr)

        return {
            "vector": [(self.s_rc, self.s_rc.v)],
            "scalar": [(self.s_tpe, self.s_tpe.v)],
            "sync": [(self.s_xld0, self.s_xld0.v), (self.s_xld1, self.s_xld1.v)],
            "gpsimd": [(self.s_misc, self.s_misc.v), (self.s_xn0, self.s_xn0.v),
                       (self.s_xn1, self.s_xn1.v), (self.s_xn2, self.s_xn2.v),
                       (self.s_xn3, self.s_xn3.v)],
        }

    def phase_b(self, bt):
        nc = self.nc
        with contextlib.ExitStack() as st:
            E = st.enter_context
            bvb = E(nc.sbuf_tensor("bvbB", [128, H], F32))
            ot = [E(nc.sbuf_tensor(f"otB{i}", [128, 512], BF16)) for i in range(4)]
            ps = [E(nc.psum_tensor(f"psB{i}", [128, 512], F32)) for i in range(4)]
            wld = [self.s_wld0, self.s_wld1]
            sp = [self.s_sp0, self.s_sp1, self.s_sp2, self.s_sp3]
            b_gmm, b_bev = self.s_gmm.v, self.s_bev.v
            b_wld = [self.s_wld0.v - 16, self.s_wld1.v - 16]
            b_sp = [s.v for s in sp]
            b_misc = self.s_misc.v

            groups = []
            for oc in range(16):
                for tc in range(4):
                    groups.append(("q", oc, tc))
            for oc in range(16):
                for tc in range(4):
                    groups.append(("k", oc, tc))
            for oc4 in range(4):
                for tt in range(16):
                    groups.append(("v", oc4, tt))
            wsched = []
            for oc in range(16):
                wsched.append((self.wq[oc].rearrange("p c o -> p (c o)"),
                               4 * oc, 4, 2048))
            for oc in range(16):
                wsched.append((self.wk[oc].rearrange("p c o -> p (c o)"),
                               64 + 4 * oc, 4, 2048))
            for oc4 in range(4):
                wsched.append((self.wv[:, :, oc4 * 512:(oc4 + 1) * 512]
                               .rearrange("c p o -> p c o"),
                               128 + 16 * oc4, 16, 8192))
            grp_w = {}
            for w, (_, g0, ng, _) in enumerate(wsched):
                for gi in range(g0, g0 + ng):
                    grp_w[gi] = w

            with nc.Block() as blk:
                @blk.gpsimd
                def _(g):
                    g.wait_ge(self.s_bar.h, bt)
                    self.s_misc.inc(g.dma_start(bvb[:, :], self.bvb_in), 16)
                    for w in range(2, len(wsched)):
                        p = w % 2
                        prev_g0, prev_ng = wsched[w - 2][1], wsched[w - 2][2]
                        g.wait_ge(self.s_gmm.h, b_gmm + prev_g0 + prev_ng)
                        nel = wsched[w][3]
                        dst = self.wbuf[p][:, 0:nel]
                        if nel == 8192:
                            dst = dst.rearrange("p (c o) -> p c o", c=16)
                        wld[p].inc(g.dma_start(dst, wsched[w][0]), 16)
                    for j4 in range(4):
                        g.wait_ge(sp[j4].h, b_sp[j4] + 16 * len(
                            [1 for gi in range(len(groups)) if gi % 4 == j4]))
                    self.s_cc.inc(g.collective_compute(
                        "AllGather", ALU.bypass,
                        replica_groups=[[0, 1], [2, 3], [4, 5], [6, 7]],
                        ins=[self.kv_own.rearrange("c p o -> (c p) o").opt()],
                        outs=[self.kv_all.rearrange("r c p o -> (r c p) o").opt()]))
                    for ic in range(2):
                        wld[ic].inc(g.dma_start(
                            self.wbuf[ic][:, 0:2048],
                            self.wgu[ic].rearrange("p c o -> p (c o)")), 16)
                        wld[ic].inc(g.dma_start(
                            self.wbuf[ic][:, 2048:4096],
                            self.wgu[64 + ic].rearrange("p c o -> p (c o)")), 16)

                @blk.tensor
                def _(t):
                    t.wait_ge(self.s_bar.h, bt)
                    for gi, grp in enumerate(groups):
                        bank = gi % 4
                        if gi >= 4:
                            t.wait_ge(self.s_bev.h, b_bev + gi - 3)
                        kind, i1, i2 = grp
                        w = grp_w[gi]
                        p = w % 2
                        t.wait_ge(wld[p].h, b_wld[p] + 16 * (w // 2 + 1))
                        if kind in ("q", "k"):
                            col = i2 * 512
                            for hc in range(NHC):
                                mm = t.matmul(
                                    ps[bank][:, :],
                                    self.wbuf[p][:, hc * 128:(hc + 1) * 128],
                                    self.xn_fm[:, hc, col:col + 512],
                                    start=(hc == 0), stop=(hc == NHC - 1))
                        else:
                            tt = i2
                            for hc in range(NHC):
                                mm = t.matmul(
                                    ps[bank][:, :],
                                    self.xn_fm[:, hc, tt * 128:(tt + 1) * 128],
                                    self.wbuf[p][:, hc * 512:(hc + 1) * 512],
                                    start=(hc == 0), stop=(hc == NHC - 1))
                        self.s_gmm.inc(mm)

                @blk.vector
                def _(v):
                    v.wait_ge(self.s_bar.h, bt)
                    v.wait_ge(self.s_misc.h, b_misc + 16)
                    for gi, grp in enumerate(groups):
                        bank = gi % 4
                        v.wait_ge(self.s_gmm.h, b_gmm + gi + 1)
                        kind, i1, i2 = grp
                        ns = gi
                        j4 = ns % 4
                        if ns >= 4:
                            v.wait_ge(sp[j4].h, b_sp[j4] + 16 * (ns // 4))
                        if kind == "q":
                            self.s_bev.inc(v.tensor_scalar(
                                ot[j4][:, :], ps[bank][:, :],
                                self.bqk_sb[:, i1:i1 + 1], None, op0=ALU.add))
                        elif kind == "k":
                            self.s_bev.inc(v.tensor_scalar(
                                ot[j4][:, :], ps[bank][:, :],
                                self.bqk_sb[:, 16 + i1:17 + i1], None, op0=ALU.add))
                        else:
                            self.s_bev.inc(v.tensor_tensor(
                                ot[j4][:, :], ps[bank][:, :],
                                bvb[:, i1 * 512:(i1 + 1) * 512], op=ALU.add))

                @blk.sync
                def _(sy):
                    sy.wait_ge(self.s_bar.h, bt)
                    for gi, grp in enumerate(groups):
                        kind, i1, i2 = grp
                        sy.wait_ge(self.s_bev.h, b_bev + gi + 1)
                        j4 = gi % 4
                        if kind == "q":
                            dst, src = self.q_d[i1, :, i2 * 512:(i2 + 1) * 512], ot[j4][:, :]
                        elif kind == "k":
                            dst, src = self.kv_own[i1, :, i2 * 512:(i2 + 1) * 512], ot[j4][:, :]
                        else:
                            dst, src = self.kv_own[16 + i2, :, i1 * 512:(i1 + 1) * 512], ot[j4][:, :]
                        sp[j4].inc(sy.dma_start(dst, src), 16)

        return {
            "vector": [(self.s_bev, self.s_bev.v)],
            "sync": [(s, s.v) for s in sp],
        }

    def phase_c(self, bt):
        nc = self.nc
        with contextlib.ExitStack() as st:
            E = st.enter_context
            silu_t = [E(nc.sbuf_tensor(f"siluC{i}", [128, 512], F32)) for i in range(2)]
            ht = [E(nc.sbuf_tensor(f"htC{i}", [128, 512], BF16)) for i in range(4)]
            ps = [E(nc.psum_tensor(f"psC{i}", [128, 512], F32)) for i in range(4)]
            wld = [self.s_wld0, self.s_wld1]
            sp = [self.s_sp0, self.s_sp1, self.s_sp2, self.s_sp3]
            b_wld = [s.v - 32 for s in wld]
            b_gmm, b_silu, b_hev = self.s_gmm.v, self.s_silu.v, self.s_hev.v
            b_sp = [s.v for s in sp]

            with nc.Block() as blk:
                @blk.gpsimd
                def _(g):
                    g.wait_ge(self.s_bar.h, bt)
                    for ic in range(2, 64):
                        p = ic % 2
                        g.wait_ge(self.s_gmm.h, b_gmm + 8 * (ic - 1))
                        wld[p].inc(g.dma_start(
                            self.wbuf[p][:, 0:2048],
                            self.wgu[ic].rearrange("p c o -> p (c o)")), 16)
                        wld[p].inc(g.dma_start(
                            self.wbuf[p][:, 2048:4096],
                            self.wgu[64 + ic].rearrange("p c o -> p (c o)")), 16)
                        if ic >= 44 and (ic - 44) % 5 == 0 and ic < 64:
                            k = (ic - 44) // 5
                            if k < 4:
                                getattr(self, "s_wd%d" % k).inc(g.dma_start(
                                    self.wd0[:, 16 * k:16 * (k + 1), :],
                                    self.wd[16 * k:16 * (k + 1), :, 0:512]
                                        .rearrange("c p o -> p c o")), 16)

                @blk.tensor
                def _(t):
                    t.wait_ge(self.s_bar.h, bt)
                    for ic in range(64):
                        p = ic % 2
                        t.wait_ge(wld[p].h, b_wld[p] + 32 * (ic // 2 + 1))
                        for tc in range(4):
                            k = ic * 4 + tc
                            for half in range(2):
                                gi = 2 * k + half
                                bank = gi % 4
                                if gi >= 4:
                                    pk = (gi - 4) // 2
                                    if (gi - 4) % 2 == 0:
                                        t.wait_ge(self.s_silu.h, b_silu + pk + 1)
                                    else:
                                        t.wait_ge(self.s_hev.h, b_hev + pk + 1)
                                off = 2048 * half
                                for hc in range(NHC):
                                    mm = t.matmul(
                                        ps[bank][:, :],
                                        self.wbuf[p][:, off + hc * 128:off + (hc + 1) * 128],
                                        self.xn_fm[:, hc, tc * 512:(tc + 1) * 512],
                                        start=(hc == 0), stop=(hc == NHC - 1))
                                self.s_gmm.inc(mm)

                @blk.scalar
                def _(s):
                    s.wait_ge(self.s_bar.h, bt)
                    for k in range(256):
                        s.wait_ge(self.s_gmm.h, b_gmm + 2 * k + 1)
                        if k >= 2:
                            s.wait_ge(self.s_hev.h, b_hev + k - 1)
                        self.s_silu.inc(s.activation(silu_t[k % 2][:, :],
                                                     ps[(2 * k) % 4][:, :], AF.Silu))

                @blk.vector
                def _(v):
                    v.wait_ge(self.s_bar.h, bt)
                    for k in range(256):
                        v.wait_ge(self.s_gmm.h, b_gmm + 2 * k + 2)
                        v.wait_ge(self.s_silu.h, b_silu + k + 1)
                        j4 = k % 4
                        if k >= 4:
                            v.wait_ge(sp[j4].h, b_sp[j4] + 16 * (k // 4))
                        self.s_hev.inc(v.tensor_tensor(
                            ht[j4][:, :], silu_t[k % 2][:, :],
                            ps[(2 * k + 1) % 4][:, :], op=ALU.mult))

                @blk.sync
                def _(sy):
                    sy.wait_ge(self.s_bar.h, bt)
                    for k in range(256):
                        ic, tc = k // 4, k % 4
                        sy.wait_ge(self.s_hev.h, b_hev + k + 1)
                        sp[k % 4].inc(sy.dma_start(
                            self.h_d[ic, :, tc * 512:(tc + 1) * 512],
                            ht[k % 4][:, :]), 16)


        return {
            "vector": [(self.s_hev, self.s_hev.v)],
            "scalar": [(self.s_silu, self.s_silu.v)],
            "sync": [(s, s.v) for s in sp],
        }

    def phase_d(self, bt):
        nc = self.nc
        with contextlib.ExitStack() as st:
            E = st.enter_context
            wd_sb = [self.wd0,
                     E(nc.sbuf_tensor("wdD1", [128, 64, 512], BF16))]
            hbuf = [E(nc.sbuf_tensor(f"hbD{i}", [128, 64, 256], BF16)) for i in range(2)]
            gt = [E(nc.sbuf_tensor(f"gtD{i}", [128, 512], F32)) for i in range(2)]
            ps = [E(nc.psum_tensor(f"psD{i}", [128, 512], F32)) for i in range(4)]
            swd = [self.s_wd0, self.s_wd1, self.s_wd2, self.s_wd3,
                   self.s_wd4, self.s_wd5, self.s_wd6, self.s_wd7]
            hl = [self.s_hl0, self.s_hl1]
            gs = [self.s_gs0, self.s_gs1]
            b_swd = list(self._b_swd)
            b_hl = [s.v for s in hl]
            b_dmm, b_dev = self.s_dmm.v, self.s_dev.v
            b_gs = [s.v for s in gs]

            sched = []
            nload = [0, 0]
            for oc4 in range(4):
                start_th = (8 - 2 * oc4) % 8
                for j in range(8):
                    th = (start_th + j) % 8
                    b = j % 2
                    load = (oc4 == 0) or (j >= 2)
                    if load:
                        nload[b] += 1
                    sched.append(dict(oc4=oc4, j=j, th=th, b=b, load=load,
                                      hl_target=16 * nload[b]))

            with nc.Block() as blk:
                @blk.gpsimd
                def _(g):
                    g.wait_ge(self.s_bar.h, bt)
                    for oc4 in range(1, 4):
                        if oc4 >= 2:
                            g.wait_ge(self.s_dmm.h, b_dmm + 16 * (oc4 - 1))
                        for k in range(4):
                            swd[(oc4 % 2) * 4 + k].inc(g.dma_start(
                                wd_sb[oc4 % 2][:, 16 * k:16 * (k + 1), :],
                                self.wd[16 * k:16 * (k + 1), :,
                                        oc4 * 512:(oc4 + 1) * 512]
                                    .rearrange("c p o -> p c o")), 16)

                @blk.sync
                def _(sy):
                    sy.wait_ge(self.s_bar.h, bt)
                    for e in sched:
                        if not e["load"]:
                            continue
                        G = 16 * e["oc4"] + 2 * e["j"]
                        if G >= 4:
                            sy.wait_ge(self.s_dmm.h, b_dmm + G - 2)
                        hl[e["b"]].inc(sy.dma_start(
                            hbuf[e["b"]][:, :, :],
                            self.h_d[:, :, e["th"] * 256:(e["th"] + 1) * 256]
                                .rearrange("c p t -> p c t")), 16)

                @blk.tensor
                def _(t):
                    t.wait_ge(self.s_bar.h, bt)
                    for e in sched:
                        oc4, j = e["oc4"], e["j"]
                        t.wait_ge(hl[e["b"]].h, b_hl[e["b"]] + e["hl_target"])
                        for tt2 in range(2):
                            gi = 16 * oc4 + 2 * j + tt2
                            bank = gi % 4
                            if gi >= 4:
                                t.wait_ge(self.s_dev.h, b_dev + gi - 3)
                            for ic in range(64):
                                if j == 0 and tt2 == 0 and ic % 16 == 0:
                                    si = (oc4 % 2) * 4 + ic // 16
                                    t.wait_ge(swd[si].h,
                                              b_swd[si] + 16 * (oc4 // 2 + 1))
                                mm = t.matmul(
                                    ps[bank][:, :],
                                    hbuf[e["b"]][:, ic, tt2 * 128:(tt2 + 1) * 128],
                                    wd_sb[oc4 % 2][:, ic, :],
                                    start=(ic == 0), stop=(ic == 63))
                            self.s_dmm.inc(mm)

                @blk.scalar
                def _(s):
                    s.wait_ge(self.s_bar.h, bt)
                    for e in sched:
                        oc4, j, th = e["oc4"], e["j"], e["th"]
                        for tt2 in range(2):
                            gi = 16 * oc4 + 2 * j + tt2
                            s.wait_ge(self.s_dmm.h, b_dmm + gi + 1)
                            jp = gi % 2
                            if gi >= 2:
                                s.wait_ge(gs[jp].h, b_gs[jp] + 16 * (gi // 2))
                            self.s_dev.inc(s.activation(
                                gt[jp][:, :], ps[gi % 4][:, :], AF.Silu))
                            s.wait_ge(self.s_dev.h, b_dev + gi + 1)
                            gs[jp].inc(s.dma_start(
                                self.g_d[th * 2 + tt2, :,
                                         oc4 * 512:(oc4 + 1) * 512],
                                gt[jp][:, :]), 16)

        return {
            "scalar": [(s, s.v) for s in gs],
        }

    def phase_e(self, bt):
        nc = self.nc
        with contextlib.ExitStack() as st:
            E = st.enter_context
            sc_sb = [E(nc.sbuf_tensor(f"scE{i}", [128, S], F32)) for i in range(4)]
            p_sb = [E(nc.sbuf_tensor(f"pE{i}", [128, S], BF16)) for i in range(2)]
            pt_sb = E(nc.sbuf_tensor("ptE", [128, 32, 512], BF16))
            q_sb = E(nc.sbuf_tensor("qE", [128, 16, 512], BF16))
            k_sb = [E(nc.sbuf_tensor(f"kE{i}", [128, 16, 512], BF16)) for i in range(2)]
            v_sb = [E(nc.sbuf_tensor(f"vE{i}", [128, 4, 512], BF16)) for i in range(3)]
            ctm = E(nc.sbuf_tensor("ctmE", [128, 4, H], BF16))
            cfm = E(nc.sbuf_tensor("cfmE", [128, 16, 512], BF16))
            mxp = E(nc.sbuf_tensor("mxpE", [128, 4, 8], F32))
            mneg = E(nc.sbuf_tensor("mnegE", [128, 4], F32))
            ebias = E(nc.sbuf_tensor("ebiasE", [128, 4], F32))
            lsum = E(nc.sbuf_tensor("lsumE", [128, 4], F32))
            rinv = E(nc.sbuf_tensor("rinvE", [128, 4], F32))
            pool6 = E(nc.psum_tensor("pool6E", [128, 3072], F32))
            pt_ps = [E(nc.psum_tensor(f"ptpsE{i}", [128, 1024], BF16)) for i in range(2)]

            def sc_ps(gi):
                b = gi % 3
                return pool6[:, b * 512:(b + 1) * 512]

            def pv_ps(qt):
                return pool6[:, (2 + qt) * 512:(3 + qt) * 512]
            kl = [self.s_kl0, self.s_kl1]
            vl = [self.s_vl0, self.s_vl1, self.s_vl2]
            sb = {n: getattr(self, "s_" + n).v for n in
                  ("ql", "kl0", "kl1", "vl0", "vl1", "vl2", "cc", "scmm", "scev", "exp", "eb",
                   "rin", "ptf", "ctf", "ptev", "pvev", "vused", "cs", "mn",
                   "wox")}

            ptev_after_ctx = {}
            ptev_after_pair = {}
            f = 0
            for qb in range(5):
                if qb < 4:
                    f += 16
                    ptev_after_pair[(qb, 1)] = f
                if qb > 0:
                    f += 8
                    ptev_after_ctx[qb - 1] = f

            with nc.Block() as blk:
                @blk.gpsimd
                def _(g):
                    g.wait_ge(self.s_bar.h, bt)
                    self.s_wox.inc(g.dma_start(
                        self.wo_bf_d, self.wo.rearrange("c p o -> (c p) o"),
                        max_dma_last_dim=8192), 16)

                @blk.sync
                def _(sy):
                    sy.wait_ge(self.s_bar.h, bt)
                    for qb in range(5):
                        def kload(kc):
                            L = qb * 8 + kc
                            p = L % 2
                            if L >= 2:
                                sy.wait_ge(self.s_scmm.h, sb["scmm"] + 4 * (L - 1))
                            kl[p].inc(sy.dma_start(
                                k_sb[p][:, :, :],
                                self.kv_all[kc // 4, 0:16, :,
                                            (kc % 4) * 512:(kc % 4 + 1) * 512]
                                    .rearrange("c p t -> p c t")), 16)

                        if qb < 4:
                            if qb == 0:
                                sy.wait_ge(self.s_cc.h, self.s_cc.v)
                            if qb > 0:
                                sy.wait_ge(self.s_scmm.h, sb["scmm"] + 32 * qb)
                            self.s_ql.inc(sy.dma_start(
                                q_sb[:, :, :],
                                self.q_d[:, :, qb * 512:(qb + 1) * 512]
                                    .rearrange("c p q -> p c q")), 16)
                        if qb == 4:
                            sy.wait_ge(self.s_ptev.h, sb["ptev"] + ptev_after_ctx[3])
                            self.s_cs.inc(sy.dma_start(
                                self.c_d[:, :, 3 * 512:4 * 512]
                                    .rearrange("c p q -> p c q"), cfm[:, :, :]), 16)
                            break
                        for kc in range(2, 8):
                            kload(kc)
                        if qb > 0:
                            sy.wait_ge(self.s_ptev.h, sb["ptev"] + ptev_after_ctx[qb - 1])
                            self.s_cs.inc(sy.dma_start(
                                self.c_d[:, :, (qb - 1) * 512:qb * 512]
                                    .rearrange("c p q -> p c q"), cfm[:, :, :]), 16)
                        for hq in range(4):
                            for k4 in range(8):
                                L = (qb * 4 + hq) * 8 + k4
                                p = L % 3
                                if L >= 3:
                                    sy.wait_ge(self.s_vused.h, sb["vused"] + L - 2)
                                r0 = (4 * k4) // 16
                                row = (4 * k4) % 16
                                vl[p].inc(sy.dma_start(
                                    v_sb[p][:, :, :],
                                    self.kv_all[r0, 16 + row:20 + row, :,
                                                hq * 512:(hq + 1) * 512]
                                        .rearrange("k p o -> p k o")), 16)

                @blk.tensor
                def _(t):
                    t.wait_ge(self.s_bar.h, bt)
                    fill = [0]

                    def ctx_fills(t, qb):
                        t.wait_ge(self.s_pvev.h, sb["pvev"] + 16 * qb)
                        for qt in range(4):
                            for a in range(2):
                                if fill[0] >= 2:
                                    t.wait_ge(self.s_ptev.h, sb["ptev"] + fill[0] - 1)
                                for j in range(8):
                                    hc = a * 8 + j
                                    tr = t.transpose(
                                        pt_ps[fill[0] % 2][:, j * 128:(j + 1) * 128],
                                        ctm[:, qt, hc * 128:(hc + 1) * 128],
                                        self.ident[:, :])
                                self.s_ctf.inc(tr)
                                fill[0] += 1

                    for qb in range(5):
                        if qb == 4:
                            ctx_fills(t, qb)
                            break
                        for kc in range(8):
                            L = qb * 8 + kc
                            p = L % 2
                            t.wait_ge(kl[p].h, sb["kl%d" % p] + 16 * (L // 2 + 1))
                            if kc == 0:
                                t.wait_ge(self.s_ql.h, sb["ql"] + 16 * (qb + 1))
                                t.wait_ge(self.s_pvev.h, sb["pvev"] + 16 * qb)
                            for qt in range(4):
                                gi = qb * 32 + kc * 4 + qt
                                if gi >= 3:
                                    t.wait_ge(self.s_scev.h, sb["scev"] + gi - 2)
                                for hc in range(NHC):
                                    mm = t.matmul(
                                        sc_ps(gi),
                                        q_sb[:, hc, qt * 128:(qt + 1) * 128],
                                        k_sb[p][:, hc, :],
                                        start=(hc == 0), stop=(hc == NHC - 1))
                                self.s_scmm.inc(mm)
                        for qt in range(4):
                            t.wait_ge(self.s_exp.h, sb["exp"] + 4 * qb + qt + 1)
                            for a in range(4):
                                if fill[0] >= 2:
                                    t.wait_ge(self.s_ptev.h, sb["ptev"] + fill[0] - 1)
                                for j in range(8):
                                    kc32 = a * 8 + j
                                    tr = t.transpose(
                                        pt_ps[fill[0] % 2][:, j * 128:(j + 1) * 128],
                                        p_sb[qt % 2][:, kc32 * 128:(kc32 + 1) * 128],
                                        self.ident[:, :])
                                self.s_ptf.inc(tr)
                                fill[0] += 1
                        if qb > 0:
                            ctx_fills(t, qb)
                        t.wait_ge(self.s_ptev.h, sb["ptev"] + ptev_after_pair[(qb, 1)])
                        t.wait_ge(self.s_scev.h, sb["scev"] + 32 * (qb + 1))
                        for hq in range(4):
                            seg = qb * 4 + hq
                            if seg >= 1:
                                t.wait_ge(self.s_pvev.h, sb["pvev"] + 4 * seg)
                            for k4 in range(8):
                                L = seg * 8 + k4
                                p = L % 3
                                t.wait_ge(vl[p].h, sb["vl%d" % p] + 16 * (L // 3 + 1))
                                for kk in range(4):
                                    kc32 = 4 * k4 + kk
                                    for qt in range(4):
                                        mm = t.matmul(
                                            pv_ps(qt),
                                            pt_sb[:, kc32, qt * 128:(qt + 1) * 128],
                                            v_sb[p][:, kk, :],
                                            start=(kc32 == 0), stop=(kc32 == 31))
                                self.s_vused.inc(mm)

                @blk.vector
                def _(v):
                    v.wait_ge(self.s_bar.h, bt)
                    fill = [0]

                    def ctx_evacs(v, qb):
                        for qt in range(4):
                            for a in range(2):
                                v.wait_ge(self.s_ctf.h,
                                          sb["ctf"] + 8 * (qb - 1) + 2 * qt + a + 1)
                                if qb >= 2 and qt == 0 and a == 0:
                                    v.wait_ge(self.s_cs.h, sb["cs"] + 16 * (qb - 1))
                                self.s_ptev.inc(v.tensor_copy(
                                    cfm[:, a * 8:(a + 1) * 8, qt * 128:(qt + 1) * 128],
                                    pt_ps[fill[0] % 2][:, :]
                                        .rearrange("p (c t) -> p c t", c=8)))
                                fill[0] += 1

                    for qb in range(5):
                        if qb == 4:
                            ctx_evacs(v, qb)
                            break
                        for kc in range(8):
                            for qt in range(4):
                                gi = qb * 32 + kc * 4 + qt
                                v.wait_ge(self.s_scmm.h, sb["scmm"] + gi + 1)
                                if qb > 0 and kc == 0:
                                    v.wait_ge(self.s_exp.h, sb["exp"] + 4 * (qb - 1) + qt + 1)
                                self.s_scev.inc(v.tensor_scalar(
                                    sc_sb[qt][:, kc * 512:(kc + 1) * 512],
                                    sc_ps(gi), 1.0, None,
                                    op0=ALU.mult, op1=ALU.max,
                                    accum_out=mxp[:, qt, kc:kc + 1]))
                        for qt in range(4):
                            v.wait_ge(self.s_scev.h, sb["scev"] + qb * 32 + 28 + qt + 1)
                            self.s_mn.inc(v.reduce_max(
                                mneg[:, qt:qt + 1], mxp[:, qt, :],
                                axis=AX, negate=True))
                            v.wait_ge(self.s_mn.h, self.s_mn.v)
                            self.s_eb.inc(v.tensor_scalar(
                                ebias[:, qt:qt + 1], mneg[:, qt:qt + 1],
                                C_SCALE, None, op0=ALU.mult))
                        for qt in range(4):
                            v.wait_ge(self.s_exp.h, sb["exp"] + 4 * qb + qt + 1)
                            self.s_rin.inc(v.reciprocal(rinv[:, qt:qt + 1],
                                                        lsum[:, qt:qt + 1]))
                            for a in range(4):
                                v.wait_ge(self.s_ptf.h,
                                          sb["ptf"] + 16 * qb + 4 * qt + a + 1)
                                if qb > 0 and qt == 0 and a == 0:
                                    v.wait_ge(self.s_vused.h, sb["vused"] + 32 * qb)
                                self.s_ptev.inc(v.tensor_copy(
                                    pt_sb[:, a * 8:(a + 1) * 8, qt * 128:(qt + 1) * 128],
                                    pt_ps[fill[0] % 2][:, :]
                                        .rearrange("p (c t) -> p c t", c=8)))
                                fill[0] += 1
                        if qb > 0:
                            ctx_evacs(v, qb)
                        v.wait_ge(self.s_rin.h, sb["rin"] + 4 * (qb + 1))
                        for hq in range(4):
                            seg = qb * 4 + hq
                            v.wait_ge(self.s_vused.h, sb["vused"] + 8 * (seg + 1))
                            if qb > 0 and hq == 0:
                                v.wait_ge(self.s_ctf.h, sb["ctf"] + 8 * qb)
                            for qt in range(4):
                                self.s_pvev.inc(v.tensor_scalar(
                                    ctm[:, qt, hq * 512:(hq + 1) * 512],
                                    pv_ps(qt),
                                    rinv[:, qt:qt + 1], None, op0=ALU.mult))

                @blk.scalar
                def _(s):
                    s.wait_ge(self.s_bar.h, bt)
                    s.wait_ge(self.s_cc.h, self.s_cc.v)
                    for qb in range(4):
                        for kc in range(2):
                            L = qb * 8 + kc
                            p = L % 2
                            if L >= 2:
                                s.wait_ge(self.s_scmm.h, sb["scmm"] + 4 * (L - 1))
                            kl[p].inc(s.dma_start(
                                k_sb[p][:, :, :],
                                self.kv_all[kc // 4, 0:16, :,
                                            (kc % 4) * 512:(kc % 4 + 1) * 512]
                                    .rearrange("c p t -> p c t")), 16)
                        for qt in range(4):
                            e = qb * 4 + qt
                            s.wait_ge(self.s_eb.h, sb["eb"] + e + 1)
                            if e >= 2:
                                s.wait_ge(self.s_ptf.h, sb["ptf"] + 4 * (e - 2) + 4)
                            if qb > 0:
                                s.wait_ge(self.s_rin.h, sb["rin"] + 4 * (qb - 1) + qt + 1)
                            self.s_exp.inc(s.activation(
                                p_sb[qt % 2][:, :], sc_sb[qt][:, :], AF.Exp,
                                bias=ebias[:, qt:qt + 1], scale=C_SCALE,
                                accum_out=lsum[:, qt:qt + 1]))

        return {
            "vector": [(self.s_ptev, self.s_ptev.v), (self.s_pvev, self.s_pvev.v)],
            "scalar": [(self.s_exp, self.s_exp.v)],
            "sync": [(self.s_cs, self.s_cs.v)],
            "gpsimd": [(self.s_wox, self.s_wox.v)],
        }

    def phase_f(self, bt):
        nc = self.nc
        with contextlib.ExitStack() as st:
            E = st.enter_context
            wo_sb = E(nc.sbuf_tensor("woF", [128, 16, H], BF16))
            cb = [E(nc.sbuf_tensor(f"cbF{i}", [128, 16, 512], BF16)) for i in range(2)]
            gbuf = [E(nc.sbuf_tensor(f"gbF{i}", [128, H], F32)) for i in range(2)]
            xbuf = [E(nc.sbuf_tensor(f"xbF{i}", [128, H], F32)) for i in range(2)]
            bob = E(nc.sbuf_tensor("bobF", [128, H], F32))
            stile = [E(nc.sbuf_tensor(f"stF{i}", [128, H], F32)) for i in range(2)]
            mtile = [E(nc.sbuf_tensor(f"mtF{i}", [128, H], F32)) for i in range(2)]
            otile = [E(nc.sbuf_tensor(f"otF{i}", [128, H], F32)) for i in range(2)]
            ps = [E(nc.psum_tensor(f"psF{i}", [128, 512], F32)) for i in range(4)]
            gl = [self.s_gl0, self.s_gl1]
            xl = [self.s_xl0, self.s_xl1]
            cl = [self.s_cl0, self.s_cl1]
            osp = [self.s_osp0, self.s_osp1]
            b_fmm, b_fev, b_ep = self.s_fmm.v, self.s_fev.v, self.s_ep.v
            b_gl = [s.v for s in gl]
            b_xl = [s.v for s in xl]
            b_cl = [s.v for s in cl]
            b_osp = [s.v for s in osp]
            b_wold, b_bold = self.s_wold.v, self.s_bold.v
            b_wold2 = self.s_wold2.v

            with nc.Block() as blk:
                @blk.gpsimd
                def _(g):
                    g.wait_ge(self.s_bar.h, bt)
                    self.s_bold.inc(g.dma_start(bob[:, :], self.bob_in), 16)

                @blk.scalar
                def _(s):
                    s.wait_ge(self.s_bar.h, bt)
                    for tile in range(16):
                        par = tile % 2
                        if tile >= 2:
                            s.wait_ge(self.s_ep.h, b_ep + 4 * (tile - 1))
                        gl[par].inc(s.dma_start(gbuf[par][:, :], self.g_d[tile]), 16)
                        xl[par].inc(s.dma_start(
                            xbuf[par][:, :],
                            self.x_all[tile * 128:(tile + 1) * 128, :]), 16)

                @blk.sync
                def _(sy):
                    sy.wait_ge(self.s_bar.h, bt)

                    wo_v = self.wo_bf_d.rearrange("(c p) o -> p c o", p=128)

                    def cbload(tb):
                        par = tb % 2
                        if tb >= 2:
                            sy.wait_ge(self.s_fmm.h, b_fmm + 16 * (tb - 1))
                        cl[par].inc(sy.dma_start(
                            cb[par][:, :, :],
                            self.c_d[:, :, tb * 512:(tb + 1) * 512]
                                .rearrange("c p q -> p c q")), 16)

                    def spill(tile):
                        par = tile % 2
                        sy.wait_ge(self.s_ep.h, b_ep + 4 * (tile + 1))
                        osp[par].inc(sy.dma_start(
                            self.out_d[tile * 128:(tile + 1) * 128, :],
                            otile[par][:, :]), 16)

                    self.s_wold.inc(sy.dma_start(wo_sb[:, 0:8, :], wo_v[:, 0:8, :]), 16)
                    cbload(0)
                    self.s_wold2.inc(sy.dma_start(wo_sb[:, 8:16, :], wo_v[:, 8:16, :]), 16)
                    cbload(1)
                    spill(0)
                    spill(1)
                    cbload(2)
                    for t in range(2, 6):
                        spill(t)
                    cbload(3)
                    for t in range(6, 16):
                        spill(t)
                    sy.wait_ge(osp[0].h, b_osp[0] + 16 * 8)
                    sy.wait_ge(osp[1].h, b_osp[1] + 16 * 8)

                @blk.tensor
                def _(t):
                    t.wait_ge(self.s_bar.h, bt)
                    for tb in range(4):
                        par = tb % 2
                        t.wait_ge(cl[par].h, b_cl[par] + 16 * (tb // 2 + 1))
                        for tt in range(4):
                            for oc4 in range(4):
                                mi = (tb * 4 + tt) * 4 + oc4
                                bank = mi % 4
                                if mi >= 4:
                                    t.wait_ge(self.s_fev.h, b_fev + 2 * mi - 6)
                                for hc in range(NHC):
                                    if mi == 0 and hc == 0:
                                        t.wait_ge(self.s_wold.h, b_wold + 16)
                                    if mi == 0 and hc == 8:
                                        t.wait_ge(self.s_wold2.h, b_wold2 + 16)
                                    mm = t.matmul(
                                        ps[bank][:, :],
                                        cb[par][:, hc, tt * 128:(tt + 1) * 128],
                                        wo_sb[:, hc, oc4 * 512:(oc4 + 1) * 512],
                                        start=(hc == 0), stop=(hc == NHC - 1))
                                self.s_fmm.inc(mm)

                @blk.vector
                def _(v):
                    v.wait_ge(self.s_bar.h, bt)
                    v.wait_ge(self.s_bold.h, b_bold + 16)
                    for tile in range(16):
                        par = tile % 2
                        v.wait_ge(gl[par].h, b_gl[par] + 16 * (tile // 2 + 1))
                        for oc4 in range(4):
                            mi = tile * 4 + oc4
                            v.wait_ge(self.s_fmm.h, b_fmm + mi + 1)
                            if tile >= 2 and oc4 == 0:
                                v.wait_ge(self.s_ep.h, b_ep + 4 * (tile - 1))
                            self.s_fev.inc(v.tensor_tensor(
                                stile[par][:, oc4 * 512:(oc4 + 1) * 512],
                                ps[mi % 4][:, :],
                                bob[:, oc4 * 512:(oc4 + 1) * 512], op=ALU.add))
                            v.wait_ge(self.s_fev.h, self.s_fev.v)
                            self.s_fev.inc(v.tensor_tensor(
                                mtile[par][:, oc4 * 512:(oc4 + 1) * 512],
                                stile[par][:, oc4 * 512:(oc4 + 1) * 512],
                                gbuf[par][:, oc4 * 512:(oc4 + 1) * 512],
                                op=ALU.mult))
                            v.wait_ge(self.s_fev.h, self.s_fev.v)
                            v.wait_ge(xl[par].h, b_xl[par] + 16 * (tile // 2 + 1))
                            if tile >= 2 and oc4 == 0:
                                v.wait_ge(osp[par].h, b_osp[par] + 16 * (tile // 2))
                            self.s_ep.inc(v.tensor_tensor(
                                otile[par][:, oc4 * 512:(oc4 + 1) * 512],
                                mtile[par][:, oc4 * 512:(oc4 + 1) * 512],
                                xbuf[par][:, oc4 * 512:(oc4 + 1) * 512],
                                op=ALU.add))

        return {}

    def build(self, upto="F"):
        nc = self.nc
        stop = "ABCDEF".index(upto)
        self._cd = contextlib.ExitStack()
        self.wd0 = self._cd.enter_context(
            nc.sbuf_tensor("wdD0", [128, 64, 512], BF16))
        with contextlib.ExitStack() as st:
            self.xn_fm = st.enter_context(nc.sbuf_tensor("xn_fm", [128, NHC, T], BF16))
            self.wbuf = [st.enter_context(nc.sbuf_tensor(f"wbuf{i}", [128, 8192], BF16))
                         for i in range(2)]
            w = self.phase_a(0)
            bt = self.barrier(w)
            if self.debug:
                with nc.Block() as blk:
                    @blk.gpsimd
                    def _(g):
                        g.wait_ge(self.s_bar.h, bt)
                        self.s_misc.inc(g.dma_start(
                            self.xn_f,
                            self.xn_fm[:, :, :].rearrange("p c s -> p (c s)"),
                            max_dma_last_dim=4096), 16)
                        g.wait_ge(self.s_misc.h, self.s_misc.v)
            if stop >= 1:
                w = self.phase_b(bt)
                bt = self.barrier(w)
            if stop >= 2:
                self._b_swd = [getattr(self, "s_wd%d" % i).v for i in range(8)]
                self._b_hl = [self.s_hl0.v, self.s_hl1.v]
                w = self.phase_c(bt)
                bt = self.barrier(w)
        if stop >= 3:
            w = self.phase_d(bt)
            self._cd.close()
            bt = self.barrier(w)
        if stop >= 4:
            w = self.phase_e(bt)
            bt = self.barrier(w)
        if stop >= 5:
            self.phase_f(bt)
        if self.debug:
            with self.nc.Block() as blk:
                @blk.gpsimd
                def _(g):
                    g.wait_ge(self.s_ep.h, self.s_ep.v)
                    n = 0
                    for name, (dst, src) in self.dbg_outs.items():
                        self.s_misc.inc(g.dma_start(dst, src), 16)
                        n += 16
                    g.wait_ge(self.s_misc.h, self.s_misc.v)
        return nc


def build_program(debug=False, upto="F", fdis=()):
    nc = bass.Bass("TRN2", target_bir_lowering=False)
    with contextlib.ExitStack() as st:
        b = Builder(nc, st, debug=debug)
        b.fdis = set(fdis)
        b.build(upto=upto)
    return nc


def kernel(L0_attn_hidden_states, L0_attn_w_qkv, L0_attn_b_qkv, L0_attn_w_out,
           L0_attn_b_out, L0_ffn_w_gate_up, L0_ffn_w_down):
    x = np.ascontiguousarray(np.asarray(L0_attn_hidden_states, dtype=np.float32))
    w_qkv = np.asarray(L0_attn_w_qkv, dtype=np.float32)
    b_qkv = np.asarray(L0_attn_b_qkv, dtype=np.float32)
    w_out = np.asarray(L0_attn_w_out, dtype=np.float32)
    b_out = np.asarray(L0_attn_b_out, dtype=np.float32)
    w_gu = np.asarray(L0_ffn_w_gate_up, dtype=np.float32)
    w_dn = np.asarray(L0_ffn_w_down, dtype=np.float32)

    wq_r = np.ascontiguousarray(
        w_qkv[0:2048].reshape(16, 128, 16, 128).transpose(0, 3, 2, 1))
    wk_r = np.ascontiguousarray(
        w_qkv[2048:4096].reshape(16, 128, 16, 128).transpose(0, 3, 2, 1))
    wv_r = np.ascontiguousarray(w_qkv[4096:6144].T.reshape(16, 128, 2048))
    wgu_r = np.ascontiguousarray(
        w_gu.reshape(128, 128, 16, 128).transpose(0, 3, 2, 1))
    wd_r = np.ascontiguousarray(w_dn.T.reshape(64, 128, 2048))
    wo_r = np.ascontiguousarray(w_out.T.reshape(16, 128, 2048))
    bqk_r = np.ascontiguousarray(b_qkv[0:4096].reshape(32, 128).T)
    bvb_r = np.ascontiguousarray(np.broadcast_to(b_qkv[4096:6144], (128, 2048)))
    bob_r = np.ascontiguousarray(np.broadcast_to(b_out, (128, 2048)))

    nc = build_program()
    in_maps = []
    for c in range(8):
        bidx, half = c // 2, c % 2
        x_core = np.ascontiguousarray(x[bidx, half * 2048:(half + 1) * 2048])
        in_maps.append({
            "x_all": x_core, "wq": wq_r, "wk": wk_r, "wv": wv_r,
            "wgu": wgu_r, "wd": wd_r, "wo": wo_r,
            "bqk": bqk_r, "bvb_in": bvb_r, "bob_in": bob_r,
        })
    res = run_bass_kernel_spmd(nc, in_maps, core_ids=list(range(8)))
    out = np.empty((4, 4096, 2048), np.float32)
    for c in range(8):
        bidx, half = c // 2, c % 2
        out[bidx, half * 2048:(half + 1) * 2048, :] = res.results[c]["out"]
    return out


# version 21
# speedup vs baseline: 1.1791x; 1.1791x over previous
import contextlib
import numpy as np
import concourse.bass as bass
import concourse.mybir as mybir
from concourse.bass_utils import run_bass_kernel_spmd

F32 = mybir.dt.float32
BF16 = mybir.dt.bfloat16
AF = mybir.ActivationFunctionType
ALU = mybir.AluOpType
AX = mybir.AxisListType.X

H = 2048
NHC = 16
S = 4096
T = 2048
EPS = 1e-5
C_SCALE = float(1.0 / np.sqrt(2048.0))

ENGINES = ("tensor", "vector", "scalar", "gpsimd", "sync")

E_P = [0, 16, 40, 64, 88, 96]


class Sem:
    __slots__ = ("h", "v")

    def __init__(self, st, nc, name):
        self.h = st.enter_context(nc.semaphore(name))
        self.v = 0

    def inc(self, instr, n=1):
        instr.then_inc(self.h, n)
        self.v += n


class Builder:
    fdis = ()

    def __init__(self, nc, st, debug=False):
        self.nc = nc
        E = st.enter_context
        dbg = dict(kind="ExternalOutput") if debug else {}
        dt = nc.dram_tensor
        self.x_all = dt("x_all", [T, H], F32, kind="ExternalInput").ap()
        self.wq = dt("wq", [16, 128, 16, 128], F32, kind="ExternalInput").ap()
        self.wk = dt("wk", [16, 128, 16, 128], F32, kind="ExternalInput").ap()
        self.wv = dt("wv", [16, 128, H], F32, kind="ExternalInput").ap()
        self.wgu = dt("wgu", [128, 128, 16, 128], F32, kind="ExternalInput").ap()
        self.wd = dt("wd", [64, 128, H], F32, kind="ExternalInput").ap()
        self.wo = dt("wo", [16, 128, H], F32, kind="ExternalInput").ap()
        self.bqk = dt("bqk", [128, 32], F32, kind="ExternalInput").ap()
        self.bvb_in = dt("bvb_in", [128, H], F32, kind="ExternalInput").ap()
        self.bob_in = dt("bob_in", [128, H], F32, kind="ExternalInput").ap()
        self.out_d = dt("out", [T, H], F32, kind="ExternalOutput").ap()
        self.debug = debug
        self.q_d = dt("q_d", [16, 128, T], BF16).ap()
        self.kv_own = dt("kv_own", [32, 128, H], BF16).ap()
        self.kv_all = dt("kv_all", [2, 32, 128, H], BF16).ap()
        self.h_d = dt("h_d", [64, 128, T], BF16).ap()
        self.g_d = dt("g_d", [16, 128, H], F32, **dbg).ap()
        self.c_d = dt("c_d", [16, 128, T], BF16).ap()
        self.wo_bf_d = dt("wo_bf_d", [16 * 128, H], BF16).ap()
        if debug:
            self.xn_f = dt("xn_f", [128, NHC * T], F32, kind="ExternalOutput").ap()
            self.dbg_outs = {
                "q_f": (dt("q_f", [16, 128, T], F32, kind="ExternalOutput").ap(), self.q_d),
                "kv_f": (dt("kv_f", [2, 32, 128, H], F32, kind="ExternalOutput").ap(), self.kv_all),
                "h_f": (dt("h_f", [64, 128, T], F32, kind="ExternalOutput").ap(), self.h_d),
                "c_f": (dt("c_f", [16, 128, T], F32, kind="ExternalOutput").ap(), self.c_d),
            }
        self.ident = E(nc.sbuf_tensor("ident", [128, 128], BF16))
        self.epsv = E(nc.sbuf_tensor("epsv", [128, 1], F32))
        self.bqk_sb = E(nc.sbuf_tensor("bqk_sb", [128, 32], F32))
        for name in ("bar", "bnA", "rc", "mn", "mset", "wox", "nm0", "nm1", "vl2", "cc",
                     "xld2", "xld3", "xn2", "xn3",
                     "xld0", "xld1", "xn0", "xn1", "mv0", "mv1", "sd0", "sd1",
                     "tp", "tpe", "wld0", "wld1", "misc", "gmm", "bev",
                     "sp0", "sp1", "sp2", "sp3", "silu", "hev",
                     "wd0", "wd1", "wd2", "wd3", "wd4", "wd5", "wd6", "wd7",
                     "dmm", "dev", "gs0", "gs1",
                     "hl0", "hl1", "ql", "kl0", "kl1", "vl0", "vl1",
                     "scmm", "scev", "exp", "eb", "rin", "ptf", "ctf", "ptev",
                     "pvev", "vused", "cs", "wold", "wold2", "bold", "fmm", "fev", "ep",
                     "gl0", "gl1", "xl0", "xl1", "cl0", "cl1", "osp0", "osp1"):
            setattr(self, "s_" + name, Sem(st, nc, "s_" + name))

    def barrier(self, waits):
        with self.nc.Block() as blk:
            for en in ENGINES:
                dec = getattr(blk, en)

                def body(e, en=en):
                    for (s, v) in waits.get(en, ()):
                        e.wait_ge(s.h, v)
                    e.nop().then_inc(self.s_bar.h, 1)

                dec(body)
        self.s_bar.v += 5
        return self.s_bar.v

    def phase_a(self, bt):
        nc = self.nc
        with contextlib.ExitStack() as st:
            E = st.enter_context
            xbuf = [E(nc.sbuf_tensor(f"xbufA{i}", [128, H], F32)) for i in range(4)]
            xn_tm = [E(nc.sbuf_tensor(f"xntmA{i}", [128, H], BF16)) for i in range(2)]
            bn6 = [E(nc.sbuf_tensor(f"bn6A{i}", [128, 24], F32)) for i in range(2)]
            mv = [E(nc.sbuf_tensor(f"mvA{i}", [128, 2], F32)) for i in range(2)]
            sd = [E(nc.sbuf_tensor(f"sdA{i}", [128, 1], F32)) for i in range(2)]
            rstd = [E(nc.sbuf_tensor(f"rstdA{i}", [128, 1], F32)) for i in range(2)]
            nmean = [E(nc.sbuf_tensor(f"nmA{i}", [128, 1], F32)) for i in range(2)]
            pbt = [E(nc.psum_tensor(f"pbtA{i}", [128, 1024], BF16)) for i in range(2)]
            xld = [self.s_xld0, self.s_xld1, self.s_xld2, self.s_xld3]
            xn = [self.s_xn0, self.s_xn1, self.s_xn2, self.s_xn3]
            mvs = [self.s_mv0, self.s_mv1]
            sds = [self.s_sd0, self.s_sd1]

            def evac(eng, tt):
                eng.wait_ge(self.s_tp.h, 2 * tt + 2)
                for a in range(2):
                    self.s_tpe.inc(eng.copy(
                        self.xn_fm[:, a * 8:(a + 1) * 8, tt * 128:(tt + 1) * 128],
                        pbt[a][:, :].rearrange("p (c t) -> p c t", c=8)))

            with nc.Block() as blk:
                @blk.sync
                def _(sy):
                    sy.wait_ge(self.s_bar.h, bt)
                    for tt in range(16):
                        p4, r4 = tt % 4, tt // 4
                        if r4 > 0:
                            sy.wait_ge(xn[p4].h, r4)
                        xld[p4].inc(sy.dma_start(
                            xbuf[p4][:, :],
                            self.x_all[tt * 128:(tt + 1) * 128, :]), 16)

                @blk.gpsimd
                def _(g):
                    g.wait_ge(self.s_bar.h, bt)
                    self.s_mset.inc(g.memset(self.ident[:, :], 0.0))
                    g.wait_ge(self.s_mset.h, 1)
                    g.affine_select(out=self.ident[:, :], in_=self.ident[:, :],
                                    compare_op=ALU.not_equal, fill=1.0, base=0,
                                    pattern=[[-1, 128]], channel_multiplier=1)
                    self.s_mset.inc(g.memset(self.epsv[:, :], EPS))
                    self.s_misc.inc(g.dma_start(self.bqk_sb[:, :], self.bqk), 16)
                    for oc in range(2):
                        wld = [self.s_wld0, self.s_wld1][oc]
                        wld.inc(g.dma_start(
                            self.wbuf[oc][:, 0:2048],
                            self.wq[oc].rearrange("p c o -> p (c o)")), 16)
                    nms = [self.s_nm0, self.s_nm1]
                    for tt in range(16):
                        p, r = tt % 2, tt // 2
                        p4, r4 = tt % 4, tt // 4
                        g.wait_ge(xld[p4].h, 16 * (r4 + 1))
                        g.wait_ge(nms[p].h, r + 1)
                        g.wait_ge(self.s_rc.h, tt + 1)
                        if tt >= 2:
                            g.wait_ge(self.s_tp.h, 2 * (tt - 2) + 2)
                        xn[p4].inc(g.tensor_scalar(
                            xn_tm[p][:, :], xbuf[p4][:, :],
                            nmean[p][:, :], rstd[p][:, :],
                            op0=ALU.add, op1=ALU.mult))

                @blk.vector
                def _(v):
                    v.wait_ge(self.s_bar.h, bt)
                    nms = [self.s_nm0, self.s_nm1]
                    for tt in range(16):
                        p, r = tt % 2, tt // 2
                        p4, r4 = tt % 4, tt // 4
                        v.wait_ge(xld[p4].h, 16 * (r4 + 1))
                        if tt >= 2:
                            v.wait_ge(xn[(tt - 2) % 4].h, (tt - 2) // 4 + 1)
                        for j in range(4):
                            self.s_bnA.inc(v.bn_stats(
                                bn6[p][:, 6 * j:6 * j + 6],
                                xbuf[p4][:, 512 * j:512 * (j + 1)]))
                        v.wait_ge(self.s_bnA.h, 4 * (tt + 1))
                        mvs[p].inc(v.bn_aggr(mv[p][:, :], bn6[p][:, :]))
                        v.wait_ge(mvs[p].h, r + 1)
                        nms[p].inc(v.tensor_scalar(
                            nmean[p][:, :], mv[p][:, 0:1], -1.0, None,
                            op0=ALU.mult))
                        v.wait_ge(sds[p].h, r + 1)
                        self.s_rc.inc(v.reciprocal(rstd[p][:, :], sd[p][:, :]))

                @blk.scalar
                def _(s):
                    s.wait_ge(self.s_bar.h, bt)
                    s.wait_ge(self.s_mset.h, 2)
                    for tt in range(16):
                        p, r = tt % 2, tt // 2
                        s.wait_ge(mvs[p].h, r + 1)
                        sds[p].inc(s.activation(sd[p][:, :], mv[p][:, 1:2],
                                                AF.Sqrt, bias=self.epsv[:, 0:1]))
                        if tt >= 1:
                            evac(s, tt - 1)
                    evac(s, 15)

                @blk.tensor
                def _(t):
                    t.wait_ge(self.s_bar.h, bt)
                    for tt in range(16):
                        p = tt % 2
                        t.wait_ge(xn[tt % 4].h, tt // 4 + 1)
                        if tt >= 1:
                            t.wait_ge(self.s_tpe.h, 2 * tt)
                        for a in range(2):
                            for j in range(8):
                                hc = a * 8 + j
                                tr = t.transpose(
                                    pbt[a][:, j * 128:(j + 1) * 128],
                                    xn_tm[p][:, hc * 128:(hc + 1) * 128],
                                    self.ident[:, :])
                                if j == 7:
                                    self.s_tp.inc(tr)

        return {
            "vector": [(self.s_rc, self.s_rc.v)],
            "scalar": [(self.s_tpe, self.s_tpe.v)],
            "sync": [(self.s_xld0, self.s_xld0.v), (self.s_xld1, self.s_xld1.v)],
            "gpsimd": [(self.s_misc, self.s_misc.v), (self.s_xn0, self.s_xn0.v),
                       (self.s_xn1, self.s_xn1.v), (self.s_xn2, self.s_xn2.v),
                       (self.s_xn3, self.s_xn3.v)],
        }

    def phase_b(self, bt):
        nc = self.nc
        with contextlib.ExitStack() as st:
            E = st.enter_context
            bvb = E(nc.sbuf_tensor("bvbB", [128, H], F32))
            ot = [E(nc.sbuf_tensor(f"otB{i}", [128, 512], BF16)) for i in range(4)]
            ps = [E(nc.psum_tensor(f"psB{i}", [128, 512], F32)) for i in range(4)]
            wld = [self.s_wld0, self.s_wld1]
            sp = [self.s_sp0, self.s_sp1, self.s_sp2, self.s_sp3]
            b_gmm, b_bev = self.s_gmm.v, self.s_bev.v
            b_wld = [self.s_wld0.v - 16, self.s_wld1.v - 16]
            b_sp = [s.v for s in sp]
            b_misc = self.s_misc.v

            groups = []
            for oc in range(16):
                for tc in range(4):
                    groups.append(("q", oc, tc))
            for oc in range(16):
                for tc in range(4):
                    groups.append(("k", oc, tc))
            for oc4 in range(4):
                for tt in range(16):
                    groups.append(("v", oc4, tt))
            wsched = []
            for oc in range(16):
                wsched.append((self.wq[oc].rearrange("p c o -> p (c o)"),
                               4 * oc, 4, 2048))
            for oc in range(16):
                wsched.append((self.wk[oc].rearrange("p c o -> p (c o)"),
                               64 + 4 * oc, 4, 2048))
            for oc4 in range(4):
                wsched.append((self.wv[:, :, oc4 * 512:(oc4 + 1) * 512]
                               .rearrange("c p o -> p c o"),
                               128 + 16 * oc4, 16, 8192))
            grp_w = {}
            for w, (_, g0, ng, _) in enumerate(wsched):
                for gi in range(g0, g0 + ng):
                    grp_w[gi] = w

            with nc.Block() as blk:
                @blk.gpsimd
                def _(g):
                    g.wait_ge(self.s_bar.h, bt)
                    self.s_misc.inc(g.dma_start(bvb[:, :], self.bvb_in), 16)
                    for w in range(2, len(wsched)):
                        p = w % 2
                        prev_g0, prev_ng = wsched[w - 2][1], wsched[w - 2][2]
                        g.wait_ge(self.s_gmm.h, b_gmm + prev_g0 + prev_ng)
                        nel = wsched[w][3]
                        dst = self.wbuf[p][:, 0:nel]
                        if nel == 8192:
                            dst = dst.rearrange("p (c o) -> p c o", c=16)
                        wld[p].inc(g.dma_start(dst, wsched[w][0]), 16)
                    for ic in range(2):
                        g.wait_ge(self.s_gmm.h, b_gmm + (176 if ic == 0 else 192))
                        wld[ic].inc(g.dma_start(
                            self.wbuf[ic][:, 0:2048],
                            self.wgu[ic].rearrange("p c o -> p (c o)")), 16)
                        wld[ic].inc(g.dma_start(
                            self.wbuf[ic][:, 2048:4096],
                            self.wgu[64 + ic].rearrange("p c o -> p (c o)")), 16)
                    for j4 in range(4):
                        g.wait_ge(sp[j4].h, b_sp[j4] + 16 * len(
                            [1 for gi in range(len(groups)) if gi % 4 == j4]))
                    self.s_cc.inc(g.collective_compute(
                        "AllGather", ALU.bypass,
                        replica_groups=[[0, 1], [2, 3], [4, 5], [6, 7]],
                        ins=[self.kv_own.rearrange("c p o -> (c p) o").opt()],
                        outs=[self.kv_all.rearrange("r c p o -> (r c p) o").opt()]))

                @blk.tensor
                def _(t):
                    t.wait_ge(self.s_bar.h, bt)
                    for gi, grp in enumerate(groups):
                        bank = gi % 4
                        if gi >= 4:
                            t.wait_ge(self.s_bev.h, b_bev + gi - 3)
                        kind, i1, i2 = grp
                        w = grp_w[gi]
                        p = w % 2
                        t.wait_ge(wld[p].h, b_wld[p] + 16 * (w // 2 + 1))
                        if kind in ("q", "k"):
                            col = i2 * 512
                            for hc in range(NHC):
                                mm = t.matmul(
                                    ps[bank][:, :],
                                    self.wbuf[p][:, hc * 128:(hc + 1) * 128],
                                    self.xn_fm[:, hc, col:col + 512],
                                    start=(hc == 0), stop=(hc == NHC - 1))
                        else:
                            tt = i2
                            for hc in range(NHC):
                                mm = t.matmul(
                                    ps[bank][:, :],
                                    self.xn_fm[:, hc, tt * 128:(tt + 1) * 128],
                                    self.wbuf[p][:, hc * 512:(hc + 1) * 512],
                                    start=(hc == 0), stop=(hc == NHC - 1))
                        self.s_gmm.inc(mm)

                @blk.vector
                def _(v):
                    v.wait_ge(self.s_bar.h, bt)
                    v.wait_ge(self.s_misc.h, b_misc + 16)
                    for gi, grp in enumerate(groups):
                        bank = gi % 4
                        v.wait_ge(self.s_gmm.h, b_gmm + gi + 1)
                        kind, i1, i2 = grp
                        ns = gi
                        j4 = ns % 4
                        if ns >= 4:
                            v.wait_ge(sp[j4].h, b_sp[j4] + 16 * (ns // 4))
                        if kind == "q":
                            self.s_bev.inc(v.tensor_scalar(
                                ot[j4][:, :], ps[bank][:, :],
                                self.bqk_sb[:, i1:i1 + 1], None, op0=ALU.add))
                        elif kind == "k":
                            self.s_bev.inc(v.tensor_scalar(
                                ot[j4][:, :], ps[bank][:, :],
                                self.bqk_sb[:, 16 + i1:17 + i1], None, op0=ALU.add))
                        else:
                            self.s_bev.inc(v.tensor_tensor(
                                ot[j4][:, :], ps[bank][:, :],
                                bvb[:, i1 * 512:(i1 + 1) * 512], op=ALU.add))

                @blk.sync
                def _(sy):
                    sy.wait_ge(self.s_bar.h, bt)
                    for gi, grp in enumerate(groups):
                        kind, i1, i2 = grp
                        sy.wait_ge(self.s_bev.h, b_bev + gi + 1)
                        j4 = gi % 4
                        if kind == "q":
                            dst, src = self.q_d[i1, :, i2 * 512:(i2 + 1) * 512], ot[j4][:, :]
                        elif kind == "k":
                            dst, src = self.kv_own[i1, :, i2 * 512:(i2 + 1) * 512], ot[j4][:, :]
                        else:
                            dst, src = self.kv_own[16 + i2, :, i1 * 512:(i1 + 1) * 512], ot[j4][:, :]
                        sp[j4].inc(sy.dma_start(dst, src), 16)

        return {
            "vector": [(self.s_bev, self.s_bev.v)],
            "sync": [(s, s.v) for s in sp],
        }

    def phase_c(self, bt):
        nc = self.nc
        with contextlib.ExitStack() as st:
            E = st.enter_context
            silu_t = [E(nc.sbuf_tensor(f"siluC{i}", [128, 512], F32)) for i in range(2)]
            ht = [E(nc.sbuf_tensor(f"htC{i}", [128, 512], BF16)) for i in range(4)]
            ps = [E(nc.psum_tensor(f"psC{i}", [128, 512], F32)) for i in range(4)]
            wld = [self.s_wld0, self.s_wld1]
            sp = [self.s_sp0, self.s_sp1, self.s_sp2, self.s_sp3]
            b_wld = [s.v - 32 for s in wld]
            b_gmm, b_silu, b_hev = self.s_gmm.v, self.s_silu.v, self.s_hev.v
            b_sp = [s.v for s in sp]

            with nc.Block() as blk:
                @blk.gpsimd
                def _(g):
                    g.wait_ge(self.s_bar.h, bt)
                    for ic in range(2, 64):
                        p = ic % 2
                        g.wait_ge(self.s_gmm.h, b_gmm + 8 * (ic - 1))
                        wld[p].inc(g.dma_start(
                            self.wbuf[p][:, 0:2048],
                            self.wgu[ic].rearrange("p c o -> p (c o)")), 16)
                        wld[p].inc(g.dma_start(
                            self.wbuf[p][:, 2048:4096],
                            self.wgu[64 + ic].rearrange("p c o -> p (c o)")), 16)
                        if ic >= 44 and (ic - 44) % 5 == 0 and ic < 64:
                            k = (ic - 44) // 5
                            if k < 4:
                                getattr(self, "s_wd%d" % k).inc(g.dma_start(
                                    self.wd0[:, 16 * k:16 * (k + 1), :],
                                    self.wd[16 * k:16 * (k + 1), :, 0:512]
                                        .rearrange("c p o -> p c o")), 16)

                @blk.tensor
                def _(t):
                    t.wait_ge(self.s_bar.h, bt)
                    for ic in range(64):
                        p = ic % 2
                        t.wait_ge(wld[p].h, b_wld[p] + 32 * (ic // 2 + 1))
                        for tc in range(4):
                            k = ic * 4 + tc
                            for half in range(2):
                                gi = 2 * k + half
                                bank = gi % 4
                                if gi >= 4:
                                    pk = (gi - 4) // 2
                                    if (gi - 4) % 2 == 0:
                                        t.wait_ge(self.s_silu.h, b_silu + pk + 1)
                                    else:
                                        t.wait_ge(self.s_hev.h, b_hev + pk + 1)
                                off = 2048 * half
                                for hc in range(NHC):
                                    mm = t.matmul(
                                        ps[bank][:, :],
                                        self.wbuf[p][:, off + hc * 128:off + (hc + 1) * 128],
                                        self.xn_fm[:, hc, tc * 512:(tc + 1) * 512],
                                        start=(hc == 0), stop=(hc == NHC - 1))
                                self.s_gmm.inc(mm)

                @blk.scalar
                def _(s):
                    s.wait_ge(self.s_bar.h, bt)
                    for k in range(256):
                        s.wait_ge(self.s_gmm.h, b_gmm + 2 * k + 1)
                        if k >= 2:
                            s.wait_ge(self.s_hev.h, b_hev + k - 1)
                        self.s_silu.inc(s.activation(silu_t[k % 2][:, :],
                                                     ps[(2 * k) % 4][:, :], AF.Silu))

                @blk.vector
                def _(v):
                    v.wait_ge(self.s_bar.h, bt)
                    for k in range(256):
                        v.wait_ge(self.s_gmm.h, b_gmm + 2 * k + 2)
                        v.wait_ge(self.s_silu.h, b_silu + k + 1)
                        j4 = k % 4
                        if k >= 4:
                            v.wait_ge(sp[j4].h, b_sp[j4] + 16 * (k // 4))
                        self.s_hev.inc(v.tensor_tensor(
                            ht[j4][:, :], silu_t[k % 2][:, :],
                            ps[(2 * k + 1) % 4][:, :], op=ALU.mult))

                @blk.sync
                def _(sy):
                    sy.wait_ge(self.s_bar.h, bt)
                    for k in range(256):
                        ic, tc = k // 4, k % 4
                        sy.wait_ge(self.s_hev.h, b_hev + k + 1)
                        sp[k % 4].inc(sy.dma_start(
                            self.h_d[ic, :, tc * 512:(tc + 1) * 512],
                            ht[k % 4][:, :]), 16)


        return {
            "vector": [(self.s_hev, self.s_hev.v)],
            "scalar": [(self.s_silu, self.s_silu.v)],
            "sync": [(s, s.v) for s in sp],
        }

    def phase_d(self, bt):
        nc = self.nc
        with contextlib.ExitStack() as st:
            E = st.enter_context
            wd_sb = [self.wd0,
                     E(nc.sbuf_tensor("wdD1", [128, 64, 512], BF16))]
            hbuf = [E(nc.sbuf_tensor(f"hbD{i}", [128, 64, 256], BF16)) for i in range(2)]
            gt = [E(nc.sbuf_tensor(f"gtD{i}", [128, 512], F32)) for i in range(2)]
            ps = [E(nc.psum_tensor(f"psD{i}", [128, 512], F32)) for i in range(4)]
            swd = [self.s_wd0, self.s_wd1, self.s_wd2, self.s_wd3,
                   self.s_wd4, self.s_wd5, self.s_wd6, self.s_wd7]
            hl = [self.s_hl0, self.s_hl1]
            gs = [self.s_gs0, self.s_gs1]
            b_swd = list(self._b_swd)
            b_hl = [s.v for s in hl]
            b_dmm, b_dev = self.s_dmm.v, self.s_dev.v
            b_gs = [s.v for s in gs]

            sched = []
            nload = [1, 0]
            for oc4 in range(4):
                start_th = (8 - 2 * oc4) % 8
                for j in range(8):
                    th = (start_th + j) % 8
                    b = j % 2
                    load = (oc4 == 0) or (j >= 2)
                    if load:
                        nload[b] += 1
                    sched.append(dict(oc4=oc4, j=j, th=th, b=b, load=load,
                                      hl_target=16 * nload[b]))

            with nc.Block() as blk:
                @blk.gpsimd
                def _(g):
                    g.wait_ge(self.s_bar.h, bt)
                    for oc4 in range(1, 4):
                        if oc4 >= 2:
                            g.wait_ge(self.s_dmm.h, b_dmm + 16 * (oc4 - 1))
                        for k in range(4):
                            swd[(oc4 % 2) * 4 + k].inc(g.dma_start(
                                wd_sb[oc4 % 2][:, 16 * k:16 * (k + 1), :],
                                self.wd[16 * k:16 * (k + 1), :,
                                        oc4 * 512:(oc4 + 1) * 512]
                                    .rearrange("c p o -> p c o")), 16)

                @blk.sync
                def _(sy):
                    sy.wait_ge(self.s_bar.h, bt)
                    first = True
                    for e in sched:
                        if not e["load"]:
                            continue
                        G = 16 * e["oc4"] + 2 * e["j"]
                        if G >= 4:
                            sy.wait_ge(self.s_dmm.h, b_dmm + G - 2)
                        if first:
                            hl[0].inc(sy.dma_start(
                                hbuf[0][:, 0:32, :],
                                self.h_d[0:32, :, 0:256]
                                    .rearrange("c p t -> p c t")), 16)
                            first = False
                            continue
                        hl[e["b"]].inc(sy.dma_start(
                            hbuf[e["b"]][:, :, :],
                            self.h_d[:, :, e["th"] * 256:(e["th"] + 1) * 256]
                                .rearrange("c p t -> p c t")), 16)

                @blk.tensor
                def _(t):
                    t.wait_ge(self.s_bar.h, bt)
                    for e in sched:
                        oc4, j = e["oc4"], e["j"]
                        t.wait_ge(hl[e["b"]].h, b_hl[e["b"]] + e["hl_target"])
                        for tt2 in range(2):
                            gi = 16 * oc4 + 2 * j + tt2
                            bank = gi % 4
                            if gi >= 4:
                                t.wait_ge(self.s_dev.h, b_dev + gi - 3)
                            for ic in range(64):
                                if j == 0 and tt2 == 0 and ic % 16 == 0:
                                    si = (oc4 % 2) * 4 + ic // 16
                                    t.wait_ge(swd[si].h,
                                              b_swd[si] + 16 * (oc4 // 2 + 1))
                                mm = t.matmul(
                                    ps[bank][:, :],
                                    hbuf[e["b"]][:, ic, tt2 * 128:(tt2 + 1) * 128],
                                    wd_sb[oc4 % 2][:, ic, :],
                                    start=(ic == 0), stop=(ic == 63))
                            self.s_dmm.inc(mm)

                @blk.scalar
                def _(s):
                    s.wait_ge(self.s_bar.h, bt)
                    hl[0].inc(s.dma_start(
                        hbuf[0][:, 32:64, :],
                        self.h_d[32:64, :, 0:256].rearrange("c p t -> p c t")), 16)
                    for e in sched:
                        oc4, j, th = e["oc4"], e["j"], e["th"]
                        for tt2 in range(2):
                            gi = 16 * oc4 + 2 * j + tt2
                            s.wait_ge(self.s_dmm.h, b_dmm + gi + 1)
                            jp = gi % 2
                            if gi >= 2:
                                s.wait_ge(gs[jp].h, b_gs[jp] + 16 * (gi // 2))
                            self.s_dev.inc(s.activation(
                                gt[jp][:, :], ps[gi % 4][:, :], AF.Silu))
                            s.wait_ge(self.s_dev.h, b_dev + gi + 1)
                            gs[jp].inc(s.dma_start(
                                self.g_d[th * 2 + tt2, :,
                                         oc4 * 512:(oc4 + 1) * 512],
                                gt[jp][:, :]), 16)

        return {
            "scalar": [(s, s.v) for s in gs],
        }

    def phase_e(self, bt):
        nc = self.nc
        with contextlib.ExitStack() as st:
            E = st.enter_context
            sc_sb = [E(nc.sbuf_tensor(f"scE{i}", [128, S], F32)) for i in range(4)]
            p_sb = [E(nc.sbuf_tensor(f"pE{i}", [128, S], BF16)) for i in range(2)]
            pt_sb = E(nc.sbuf_tensor("ptE", [128, 32, 512], BF16))
            q_sb = E(nc.sbuf_tensor("qE", [128, 16, 512], BF16))
            k_sb = [E(nc.sbuf_tensor(f"kE{i}", [128, 16, 512], BF16)) for i in range(2)]
            v_sb = [E(nc.sbuf_tensor(f"vE{i}", [128, 4, 512], BF16)) for i in range(3)]
            ctm = E(nc.sbuf_tensor("ctmE", [128, 4, H], BF16))
            cfm = E(nc.sbuf_tensor("cfmE", [128, 16, 512], BF16))
            mxp = E(nc.sbuf_tensor("mxpE", [128, 4, 8], F32))
            mneg = E(nc.sbuf_tensor("mnegE", [128, 4], F32))
            ebias = E(nc.sbuf_tensor("ebiasE", [128, 4], F32))
            lsum = E(nc.sbuf_tensor("lsumE", [128, 4], F32))
            rinv = E(nc.sbuf_tensor("rinvE", [128, 4], F32))
            pool6 = E(nc.psum_tensor("pool6E", [128, 3072], F32))
            pt_ps = [E(nc.psum_tensor(f"ptpsE{i}", [128, 1024], BF16)) for i in range(2)]

            def sc_ps(gi):
                b = gi % 3
                return pool6[:, b * 512:(b + 1) * 512]

            def pv_ps(qt):
                return pool6[:, (2 + qt) * 512:(3 + qt) * 512]
            kl = [self.s_kl0, self.s_kl1]
            vl = [self.s_vl0, self.s_vl1, self.s_vl2]
            sb = {n: getattr(self, "s_" + n).v for n in
                  ("ql", "kl0", "kl1", "vl0", "vl1", "vl2", "cc", "scmm", "scev", "exp", "eb",
                   "rin", "ptf", "ctf", "ptev", "pvev", "vused", "cs", "mn",
                   "wox")}

            ptev_after_ctx = {}
            ptev_after_pair = {}
            f = 0
            for qb in range(5):
                if qb < 4:
                    f += 16
                    ptev_after_pair[(qb, 1)] = f
                if qb > 0:
                    f += 8
                    ptev_after_ctx[qb - 1] = f

            with nc.Block() as blk:
                @blk.gpsimd
                def _(g):
                    g.wait_ge(self.s_bar.h, bt)
                    g.wait_ge(self.s_cc.h, self.s_cc.v)
                    kl[1].inc(g.dma_start(
                        k_sb[1][:, :, :],
                        self.kv_all[0, 0:16, :, 512:1024]
                            .rearrange("c p t -> p c t")), 16)
                    self.s_wox.inc(g.dma_start(
                        self.wo_bf_d, self.wo.rearrange("c p o -> (c p) o"),
                        max_dma_last_dim=8192), 16)

                @blk.sync
                def _(sy):
                    sy.wait_ge(self.s_bar.h, bt)
                    for qb in range(5):
                        def kload(kc):
                            L = qb * 8 + kc
                            p = L % 2
                            if L >= 2:
                                sy.wait_ge(self.s_scmm.h, sb["scmm"] + 4 * (L - 1))
                            kl[p].inc(sy.dma_start(
                                k_sb[p][:, :, :],
                                self.kv_all[kc // 4, 0:16, :,
                                            (kc % 4) * 512:(kc % 4 + 1) * 512]
                                    .rearrange("c p t -> p c t")), 16)

                        if qb < 4:
                            if qb == 0:
                                sy.wait_ge(self.s_cc.h, self.s_cc.v)
                            if qb > 0:
                                sy.wait_ge(self.s_scmm.h, sb["scmm"] + 32 * qb)
                            self.s_ql.inc(sy.dma_start(
                                q_sb[:, :, :],
                                self.q_d[:, :, qb * 512:(qb + 1) * 512]
                                    .rearrange("c p q -> p c q")), 16)
                        if qb == 4:
                            sy.wait_ge(self.s_ptev.h, sb["ptev"] + ptev_after_ctx[3])
                            self.s_cs.inc(sy.dma_start(
                                self.c_d[:, :, 3 * 512:4 * 512]
                                    .rearrange("c p q -> p c q"), cfm[:, :, :]), 16)
                            break
                        for kc in range(2, 8):
                            kload(kc)
                        if qb > 0:
                            sy.wait_ge(self.s_ptev.h, sb["ptev"] + ptev_after_ctx[qb - 1])
                            self.s_cs.inc(sy.dma_start(
                                self.c_d[:, :, (qb - 1) * 512:qb * 512]
                                    .rearrange("c p q -> p c q"), cfm[:, :, :]), 16)
                        for hq in range(4):
                            for k4 in range(8):
                                L = (qb * 4 + hq) * 8 + k4
                                p = L % 3
                                if L >= 3:
                                    sy.wait_ge(self.s_vused.h, sb["vused"] + L - 2)
                                r0 = (4 * k4) // 16
                                row = (4 * k4) % 16
                                vl[p].inc(sy.dma_start(
                                    v_sb[p][:, :, :],
                                    self.kv_all[r0, 16 + row:20 + row, :,
                                                hq * 512:(hq + 1) * 512]
                                        .rearrange("k p o -> p k o")), 16)

                @blk.tensor
                def _(t):
                    t.wait_ge(self.s_bar.h, bt)
                    fill = [0]

                    def ctx_fills(t, qb):
                        t.wait_ge(self.s_pvev.h, sb["pvev"] + 16 * qb)
                        for qt in range(4):
                            for a in range(2):
                                if fill[0] >= 2:
                                    t.wait_ge(self.s_ptev.h, sb["ptev"] + fill[0] - 1)
                                for j in range(8):
                                    hc = a * 8 + j
                                    tr = t.transpose(
                                        pt_ps[fill[0] % 2][:, j * 128:(j + 1) * 128],
                                        ctm[:, qt, hc * 128:(hc + 1) * 128],
                                        self.ident[:, :])
                                self.s_ctf.inc(tr)
                                fill[0] += 1

                    for qb in range(5):
                        if qb == 4:
                            ctx_fills(t, qb)
                            break
                        for kc in range(8):
                            L = qb * 8 + kc
                            p = L % 2
                            t.wait_ge(kl[p].h, sb["kl%d" % p] + 16 * (L // 2 + 1))
                            if kc == 0:
                                t.wait_ge(self.s_ql.h, sb["ql"] + 16 * (qb + 1))
                                t.wait_ge(self.s_pvev.h, sb["pvev"] + 16 * qb)
                            for qt in range(4):
                                gi = qb * 32 + kc * 4 + qt
                                if gi >= 3:
                                    t.wait_ge(self.s_scev.h, sb["scev"] + gi - 2)
                                for hc in range(NHC):
                                    mm = t.matmul(
                                        sc_ps(gi),
                                        q_sb[:, hc, qt * 128:(qt + 1) * 128],
                                        k_sb[p][:, hc, :],
                                        start=(hc == 0), stop=(hc == NHC - 1))
                                self.s_scmm.inc(mm)
                        for qt in range(4):
                            t.wait_ge(self.s_exp.h, sb["exp"] + 4 * qb + qt + 1)
                            for a in range(4):
                                if fill[0] >= 2:
                                    t.wait_ge(self.s_ptev.h, sb["ptev"] + fill[0] - 1)
                                for j in range(8):
                                    kc32 = a * 8 + j
                                    tr = t.transpose(
                                        pt_ps[fill[0] % 2][:, j * 128:(j + 1) * 128],
                                        p_sb[qt % 2][:, kc32 * 128:(kc32 + 1) * 128],
                                        self.ident[:, :])
                                self.s_ptf.inc(tr)
                                fill[0] += 1
                        if qb > 0:
                            ctx_fills(t, qb)
                        t.wait_ge(self.s_ptev.h, sb["ptev"] + ptev_after_pair[(qb, 1)])
                        t.wait_ge(self.s_scev.h, sb["scev"] + 32 * (qb + 1))
                        for hq in range(4):
                            seg = qb * 4 + hq
                            if seg >= 1:
                                t.wait_ge(self.s_pvev.h, sb["pvev"] + 4 * seg)
                            for k4 in range(8):
                                L = seg * 8 + k4
                                p = L % 3
                                t.wait_ge(vl[p].h, sb["vl%d" % p] + 16 * (L // 3 + 1))
                                for kk in range(4):
                                    kc32 = 4 * k4 + kk
                                    for qt in range(4):
                                        mm = t.matmul(
                                            pv_ps(qt),
                                            pt_sb[:, kc32, qt * 128:(qt + 1) * 128],
                                            v_sb[p][:, kk, :],
                                            start=(kc32 == 0), stop=(kc32 == 31))
                                self.s_vused.inc(mm)

                @blk.vector
                def _(v):
                    v.wait_ge(self.s_bar.h, bt)
                    fill = [0]

                    def ctx_evacs(v, qb):
                        for qt in range(4):
                            for a in range(2):
                                v.wait_ge(self.s_ctf.h,
                                          sb["ctf"] + 8 * (qb - 1) + 2 * qt + a + 1)
                                if qb >= 2 and qt == 0 and a == 0:
                                    v.wait_ge(self.s_cs.h, sb["cs"] + 16 * (qb - 1))
                                self.s_ptev.inc(v.tensor_copy(
                                    cfm[:, a * 8:(a + 1) * 8, qt * 128:(qt + 1) * 128],
                                    pt_ps[fill[0] % 2][:, :]
                                        .rearrange("p (c t) -> p c t", c=8)))
                                fill[0] += 1

                    for qb in range(5):
                        if qb == 4:
                            ctx_evacs(v, qb)
                            break
                        for kc in range(8):
                            for qt in range(4):
                                gi = qb * 32 + kc * 4 + qt
                                v.wait_ge(self.s_scmm.h, sb["scmm"] + gi + 1)
                                if qb > 0 and kc == 0:
                                    v.wait_ge(self.s_exp.h, sb["exp"] + 4 * (qb - 1) + qt + 1)
                                self.s_scev.inc(v.tensor_scalar(
                                    sc_sb[qt][:, kc * 512:(kc + 1) * 512],
                                    sc_ps(gi), 1.0, None,
                                    op0=ALU.mult, op1=ALU.max,
                                    accum_out=mxp[:, qt, kc:kc + 1]))
                        for qt in range(4):
                            v.wait_ge(self.s_scev.h, sb["scev"] + qb * 32 + 28 + qt + 1)
                            self.s_mn.inc(v.reduce_max(
                                mneg[:, qt:qt + 1], mxp[:, qt, :],
                                axis=AX, negate=True))
                            v.wait_ge(self.s_mn.h, self.s_mn.v)
                            self.s_eb.inc(v.tensor_scalar(
                                ebias[:, qt:qt + 1], mneg[:, qt:qt + 1],
                                C_SCALE, None, op0=ALU.mult))
                        for qt in range(4):
                            v.wait_ge(self.s_exp.h, sb["exp"] + 4 * qb + qt + 1)
                            self.s_rin.inc(v.reciprocal(rinv[:, qt:qt + 1],
                                                        lsum[:, qt:qt + 1]))
                            for a in range(4):
                                v.wait_ge(self.s_ptf.h,
                                          sb["ptf"] + 16 * qb + 4 * qt + a + 1)
                                if qb > 0 and qt == 0 and a == 0:
                                    v.wait_ge(self.s_vused.h, sb["vused"] + 32 * qb)
                                self.s_ptev.inc(v.tensor_copy(
                                    pt_sb[:, a * 8:(a + 1) * 8, qt * 128:(qt + 1) * 128],
                                    pt_ps[fill[0] % 2][:, :]
                                        .rearrange("p (c t) -> p c t", c=8)))
                                fill[0] += 1
                        if qb > 0:
                            ctx_evacs(v, qb)
                        v.wait_ge(self.s_rin.h, sb["rin"] + 4 * (qb + 1))
                        for hq in range(4):
                            seg = qb * 4 + hq
                            v.wait_ge(self.s_vused.h, sb["vused"] + 8 * (seg + 1))
                            if qb > 0 and hq == 0:
                                v.wait_ge(self.s_ctf.h, sb["ctf"] + 8 * qb)
                            for qt in range(4):
                                self.s_pvev.inc(v.tensor_scalar(
                                    ctm[:, qt, hq * 512:(hq + 1) * 512],
                                    pv_ps(qt),
                                    rinv[:, qt:qt + 1], None, op0=ALU.mult))

                @blk.scalar
                def _(s):
                    s.wait_ge(self.s_bar.h, bt)
                    s.wait_ge(self.s_cc.h, self.s_cc.v)
                    for qb in range(4):
                        for kc in range(2):
                            if qb == 0 and kc == 1:
                                continue
                            L = qb * 8 + kc
                            p = L % 2
                            if L >= 2:
                                s.wait_ge(self.s_scmm.h, sb["scmm"] + 4 * (L - 1))
                            kl[p].inc(s.dma_start(
                                k_sb[p][:, :, :],
                                self.kv_all[kc // 4, 0:16, :,
                                            (kc % 4) * 512:(kc % 4 + 1) * 512]
                                    .rearrange("c p t -> p c t")), 16)
                        for qt in range(4):
                            e = qb * 4 + qt
                            s.wait_ge(self.s_eb.h, sb["eb"] + e + 1)
                            if e >= 2:
                                s.wait_ge(self.s_ptf.h, sb["ptf"] + 4 * (e - 2) + 4)
                            if qb > 0:
                                s.wait_ge(self.s_rin.h, sb["rin"] + 4 * (qb - 1) + qt + 1)
                            self.s_exp.inc(s.activation(
                                p_sb[qt % 2][:, :], sc_sb[qt][:, :], AF.Exp,
                                bias=ebias[:, qt:qt + 1], scale=C_SCALE,
                                accum_out=lsum[:, qt:qt + 1]))

        return {
            "vector": [(self.s_ptev, self.s_ptev.v), (self.s_pvev, self.s_pvev.v)],
            "scalar": [(self.s_exp, self.s_exp.v)],
            "sync": [(self.s_cs, self.s_cs.v)],
            "gpsimd": [(self.s_wox, self.s_wox.v)],
        }

    def phase_f(self, bt):
        nc = self.nc
        with contextlib.ExitStack() as st:
            E = st.enter_context
            wo_sb = E(nc.sbuf_tensor("woF", [128, 16, H], BF16))
            cb = [E(nc.sbuf_tensor(f"cbF{i}", [128, 16, 512], BF16)) for i in range(2)]
            gbuf = [E(nc.sbuf_tensor(f"gbF{i}", [128, H], F32)) for i in range(2)]
            xbuf = [E(nc.sbuf_tensor(f"xbF{i}", [128, H], F32)) for i in range(2)]
            bob = E(nc.sbuf_tensor("bobF", [128, H], F32))
            stile = [E(nc.sbuf_tensor(f"stF{i}", [128, H], F32)) for i in range(2)]
            mtile = [E(nc.sbuf_tensor(f"mtF{i}", [128, H], F32)) for i in range(2)]
            otile = [E(nc.sbuf_tensor(f"otF{i}", [128, H], F32)) for i in range(2)]
            ps = [E(nc.psum_tensor(f"psF{i}", [128, 512], F32)) for i in range(4)]
            gl = [self.s_gl0, self.s_gl1]
            xl = [self.s_xl0, self.s_xl1]
            cl = [self.s_cl0, self.s_cl1]
            osp = [self.s_osp0, self.s_osp1]
            b_fmm, b_fev, b_ep = self.s_fmm.v, self.s_fev.v, self.s_ep.v
            b_gl = [s.v for s in gl]
            b_xl = [s.v for s in xl]
            b_cl = [s.v for s in cl]
            b_osp = [s.v for s in osp]
            b_wold, b_bold = self.s_wold.v, self.s_bold.v
            b_wold2 = self.s_wold2.v

            with nc.Block() as blk:
                @blk.gpsimd
                def _(g):
                    g.wait_ge(self.s_bar.h, bt)
                    wo_v = self.wo_bf_d.rearrange("(c p) o -> p c o", p=128)
                    self.s_wold.inc(g.dma_start(wo_sb[:, 0:8, :], wo_v[:, 0:8, :]), 16)
                    self.s_wold2.inc(g.dma_start(wo_sb[:, 8:16, :], wo_v[:, 8:16, :]), 16)
                    self.s_bold.inc(g.dma_start(bob[:, :], self.bob_in), 16)

                @blk.scalar
                def _(s):
                    s.wait_ge(self.s_bar.h, bt)
                    for tile in range(16):
                        par = tile % 2
                        if tile >= 2:
                            s.wait_ge(self.s_ep.h, b_ep + 4 * (tile - 1))
                        gl[par].inc(s.dma_start(gbuf[par][:, :], self.g_d[tile]), 16)
                        xl[par].inc(s.dma_start(
                            xbuf[par][:, :],
                            self.x_all[tile * 128:(tile + 1) * 128, :]), 16)

                @blk.sync
                def _(sy):
                    sy.wait_ge(self.s_bar.h, bt)

                    def cbload(tb):
                        par = tb % 2
                        if tb >= 2:
                            sy.wait_ge(self.s_fmm.h, b_fmm + 16 * (tb - 1))
                        cl[par].inc(sy.dma_start(
                            cb[par][:, :, :],
                            self.c_d[:, :, tb * 512:(tb + 1) * 512]
                                .rearrange("c p q -> p c q")), 16)

                    def spill(tile):
                        par = tile % 2
                        sy.wait_ge(self.s_ep.h, b_ep + 4 * (tile + 1))
                        osp[par].inc(sy.dma_start(
                            self.out_d[tile * 128:(tile + 1) * 128, :],
                            otile[par][:, :]), 16)

                    cbload(0)
                    cbload(1)
                    spill(0)
                    spill(1)
                    cbload(2)
                    for t in range(2, 6):
                        spill(t)
                    cbload(3)
                    for t in range(6, 16):
                        spill(t)
                    sy.wait_ge(osp[0].h, b_osp[0] + 16 * 8)
                    sy.wait_ge(osp[1].h, b_osp[1] + 16 * 8)

                @blk.tensor
                def _(t):
                    t.wait_ge(self.s_bar.h, bt)
                    for tb in range(4):
                        par = tb % 2
                        t.wait_ge(cl[par].h, b_cl[par] + 16 * (tb // 2 + 1))
                        for tt in range(4):
                            for oc4 in range(4):
                                mi = (tb * 4 + tt) * 4 + oc4
                                bank = mi % 4
                                if mi >= 4:
                                    t.wait_ge(self.s_fev.h, b_fev + 2 * mi - 6)
                                for hc in range(NHC):
                                    if mi == 0 and hc == 0:
                                        t.wait_ge(self.s_wold.h, b_wold + 16)
                                    if mi == 0 and hc == 8:
                                        t.wait_ge(self.s_wold2.h, b_wold2 + 16)
                                    mm = t.matmul(
                                        ps[bank][:, :],
                                        cb[par][:, hc, tt * 128:(tt + 1) * 128],
                                        wo_sb[:, hc, oc4 * 512:(oc4 + 1) * 512],
                                        start=(hc == 0), stop=(hc == NHC - 1))
                                self.s_fmm.inc(mm)

                @blk.vector
                def _(v):
                    v.wait_ge(self.s_bar.h, bt)
                    v.wait_ge(self.s_bold.h, b_bold + 16)
                    for tile in range(16):
                        par = tile % 2
                        v.wait_ge(gl[par].h, b_gl[par] + 16 * (tile // 2 + 1))
                        for oc4 in range(4):
                            mi = tile * 4 + oc4
                            v.wait_ge(self.s_fmm.h, b_fmm + mi + 1)
                            if tile >= 2 and oc4 == 0:
                                v.wait_ge(self.s_ep.h, b_ep + 4 * (tile - 1))
                            self.s_fev.inc(v.tensor_tensor(
                                stile[par][:, oc4 * 512:(oc4 + 1) * 512],
                                ps[mi % 4][:, :],
                                bob[:, oc4 * 512:(oc4 + 1) * 512], op=ALU.add))
                            v.wait_ge(self.s_fev.h, self.s_fev.v)
                            self.s_fev.inc(v.tensor_tensor(
                                mtile[par][:, oc4 * 512:(oc4 + 1) * 512],
                                stile[par][:, oc4 * 512:(oc4 + 1) * 512],
                                gbuf[par][:, oc4 * 512:(oc4 + 1) * 512],
                                op=ALU.mult))
                            v.wait_ge(self.s_fev.h, self.s_fev.v)
                            v.wait_ge(xl[par].h, b_xl[par] + 16 * (tile // 2 + 1))
                            if tile >= 2 and oc4 == 0:
                                v.wait_ge(osp[par].h, b_osp[par] + 16 * (tile // 2))
                            self.s_ep.inc(v.tensor_tensor(
                                otile[par][:, oc4 * 512:(oc4 + 1) * 512],
                                mtile[par][:, oc4 * 512:(oc4 + 1) * 512],
                                xbuf[par][:, oc4 * 512:(oc4 + 1) * 512],
                                op=ALU.add))

        return {}

    def build(self, upto="F"):
        nc = self.nc
        stop = "ABCDEF".index(upto)
        self._cd = contextlib.ExitStack()
        self.wd0 = self._cd.enter_context(
            nc.sbuf_tensor("wdD0", [128, 64, 512], BF16))
        with contextlib.ExitStack() as st:
            self.xn_fm = st.enter_context(nc.sbuf_tensor("xn_fm", [128, NHC, T], BF16))
            self.wbuf = [st.enter_context(nc.sbuf_tensor(f"wbuf{i}", [128, 8192], BF16))
                         for i in range(2)]
            w = self.phase_a(0)
            bt = self.barrier(w)
            if self.debug:
                with nc.Block() as blk:
                    @blk.gpsimd
                    def _(g):
                        g.wait_ge(self.s_bar.h, bt)
                        self.s_misc.inc(g.dma_start(
                            self.xn_f,
                            self.xn_fm[:, :, :].rearrange("p c s -> p (c s)"),
                            max_dma_last_dim=4096), 16)
                        g.wait_ge(self.s_misc.h, self.s_misc.v)
            if stop >= 1:
                w = self.phase_b(bt)
                bt = self.barrier(w)
            if stop >= 2:
                self._b_swd = [getattr(self, "s_wd%d" % i).v for i in range(8)]
                self._b_hl = [self.s_hl0.v, self.s_hl1.v]
                w = self.phase_c(bt)
                bt = self.barrier(w)
        if stop >= 3:
            w = self.phase_d(bt)
            self._cd.close()
            bt = self.barrier(w)
        if stop >= 4:
            w = self.phase_e(bt)
            bt = self.barrier(w)
        if stop >= 5:
            self.phase_f(bt)
        if self.debug:
            with self.nc.Block() as blk:
                @blk.gpsimd
                def _(g):
                    g.wait_ge(self.s_ep.h, self.s_ep.v)
                    n = 0
                    for name, (dst, src) in self.dbg_outs.items():
                        self.s_misc.inc(g.dma_start(dst, src), 16)
                        n += 16
                    g.wait_ge(self.s_misc.h, self.s_misc.v)
        return nc


def build_program(debug=False, upto="F", fdis=()):
    nc = bass.Bass("TRN2", target_bir_lowering=False)
    with contextlib.ExitStack() as st:
        b = Builder(nc, st, debug=debug)
        b.fdis = set(fdis)
        b.build(upto=upto)
    return nc


def kernel(L0_attn_hidden_states, L0_attn_w_qkv, L0_attn_b_qkv, L0_attn_w_out,
           L0_attn_b_out, L0_ffn_w_gate_up, L0_ffn_w_down):
    x = np.ascontiguousarray(np.asarray(L0_attn_hidden_states, dtype=np.float32))
    w_qkv = np.asarray(L0_attn_w_qkv, dtype=np.float32)
    b_qkv = np.asarray(L0_attn_b_qkv, dtype=np.float32)
    w_out = np.asarray(L0_attn_w_out, dtype=np.float32)
    b_out = np.asarray(L0_attn_b_out, dtype=np.float32)
    w_gu = np.asarray(L0_ffn_w_gate_up, dtype=np.float32)
    w_dn = np.asarray(L0_ffn_w_down, dtype=np.float32)

    wq_r = np.ascontiguousarray(
        w_qkv[0:2048].reshape(16, 128, 16, 128).transpose(0, 3, 2, 1))
    wk_r = np.ascontiguousarray(
        w_qkv[2048:4096].reshape(16, 128, 16, 128).transpose(0, 3, 2, 1))
    wv_r = np.ascontiguousarray(w_qkv[4096:6144].T.reshape(16, 128, 2048))
    wgu_r = np.ascontiguousarray(
        w_gu.reshape(128, 128, 16, 128).transpose(0, 3, 2, 1))
    wd_r = np.ascontiguousarray(w_dn.T.reshape(64, 128, 2048))
    wo_r = np.ascontiguousarray(w_out.T.reshape(16, 128, 2048))
    bqk_r = np.ascontiguousarray(b_qkv[0:4096].reshape(32, 128).T)
    bvb_r = np.ascontiguousarray(np.broadcast_to(b_qkv[4096:6144], (128, 2048)))
    bob_r = np.ascontiguousarray(np.broadcast_to(b_out, (128, 2048)))

    nc = build_program()
    in_maps = []
    for c in range(8):
        bidx, half = c // 2, c % 2
        x_core = np.ascontiguousarray(x[bidx, half * 2048:(half + 1) * 2048])
        in_maps.append({
            "x_all": x_core, "wq": wq_r, "wk": wk_r, "wv": wv_r,
            "wgu": wgu_r, "wd": wd_r, "wo": wo_r,
            "bqk": bqk_r, "bvb_in": bvb_r, "bob_in": bob_r,
        })
    res = run_bass_kernel_spmd(nc, in_maps, core_ids=list(range(8)))
    out = np.empty((4, 4096, 2048), np.float32)
    for c in range(8):
        bidx, half = c // 2, c % 2
        out[bidx, half * 2048:(half + 1) * 2048, :] = res.results[c]["out"]
    return out
